# version 01
# baseline (speedup 1.0000x reference)
import numpy as np
import ml_dtypes
import concourse.bass as bass
import concourse.mybir as mybir
from concourse.bass_utils import run_bass_kernel_spmd

F32 = mybir.dt.float32
BF16 = mybir.dt.bfloat16
AF = mybir.ActivationFunctionType
ALU = mybir.AluOpType
AX = mybir.AxisListType
BF = ml_dtypes.bfloat16

T = 2048
D = 1024
NCH = 8
NCORE = 8
NTT = T // 128

ENGINES = ("pe", "dve", "act", "pool", "sp")
_BLOCK_METHOD = {"pe": "tensor", "dve": "vector", "act": "scalar",
                 "pool": "gpsimd", "sp": "sync"}


class View:
    __slots__ = ("buf", "ap", "p0", "p1", "f0", "f1")

    def __init__(self, buf, ap, p0, p1, f0, f1):
        self.buf, self.ap = buf, ap
        self.p0, self.p1, self.f0, self.f1 = p0, p1, f0, f1


class Buf:

    def __init__(self, name, P, F, dtype, psum=False):
        self.name, self.P, self.F, self.dtype, self.psum = name, P, F, dtype, psum
        self.th = None

    def base(self):
        return self.th.ap()

    def _mk(self, ap, p0, p1, f0, f1):
        if self.psum:
            return View(self, ap, 0, self.P, 0, self.F)
        return View(self, ap, p0, p1, f0, f1)

    def f(self, f0=0, f1=None, p0=0, p1=None):
        if f1 is None:
            f1 = self.F
        if p1 is None:
            p1 = self.P
        return self._mk(self.base()[p0:p1, f0:f1], p0, p1, f0, f1)

    def t(self, ch, t0=0, t1=T, p0=0, p1=None):
        return self.f(ch * T + t0, ch * T + t1, p0, p1)

    def c(self, ch, w, c0, c1, p0=0, p1=None):
        return self.f(ch * w + c0, ch * w + c1, p0, p1)

    def wv(self, ap, p0, p1, f0, f1):
        return self._mk(ap, p0, p1, f0, f1)


class DBuf:
    def __init__(self, name, shape, dtype, kind):
        self.name, self.shape, self.dtype, self.kind = name, list(shape), dtype, kind
        self.size = int(np.prod(shape))
        self.th = None

    def base(self):
        return self.th.ap()

    def flat(self):
        ap = self.base()
        while ap.ndim > 1:
            ap = ap.flatten_outer_dims() if False else ap.rearrange(
                " ".join(f"a{i}" for i in range(ap.ndim)) + " -> (" +
                " ".join(f"a{i}" for i in range(ap.ndim)) + ")")
        return ap

    def whole(self):
        return View(self, self.base(), 0, 1, 0, self.size)

    def wv(self, ap, f0=0, f1=None):
        return View(self, ap, 0, 1, f0, self.size if f1 is None else f1)


class _Op:
    __slots__ = ("fn", "waits", "inc", "kind", "seq", "tag")

    def __init__(self, fn, kind, seq, tag=""):
        self.fn, self.kind, self.seq, self.tag = fn, kind, seq, tag
        self.waits = {}
        self.inc = False


class KB:
    def __init__(self, nc=None):
        import contextlib
        self.nc = nc if nc is not None else bass.Bass(
            "TRN2", target_bir_lowering=False)
        self.stack = contextlib.ExitStack()
        self.ops = {e: [] for e in ENGINES}
        self.ndma = {e: 0 for e in ENGINES}
        self.bufs = []
        self.dbufs = []
        self.mem = {}
        self.clock = {e: {} for e in ENGINES}

    def sbuf(self, name, P, F, dtype):
        b = Buf(name, P, F, dtype)
        b.th = self.stack.enter_context(self.nc.sbuf_tensor(name, [P, F], dtype))
        self.bufs.append(b)
        return b

    def psumb(self, name, P, F, dtype):
        b = Buf(name, P, F, dtype, psum=True)
        b.th = self.stack.enter_context(self.nc.psum_tensor(name, [P, F], dtype))
        self.bufs.append(b)
        return b

    def dram(self, name, shape, dtype, kind="ExternalInput"):
        d = DBuf(name, shape, dtype, kind)
        d.th = self.nc.dram_tensor(name, d.shape, dtype, kind=kind)
        self.dbufs.append(d)
        return d

    def _emit(self, eng, fn, reads, writes, kind="c", tag=""):
        ops = self.ops[eng]
        seq = len(ops)
        op = _Op(fn, kind, seq, tag)
        if kind == "d":
            self.ndma[eng] += 1

        def need(peng, pkind, pseq):
            if peng == eng and pkind == "c" and kind == "c" and eng == "pe":
                return
            ck = (peng, pkind)
            if peng != eng and pseq <= self.clock[eng].get(ck, -1):
                return
            if op.waits.get(ck, -1) < pseq:
                op.waits[ck] = pseq

        def ov(a, b):
            return a[0] < b[1] and b[0] < a[1] and a[2] < b[3] and b[2] < a[3]

        for v in reads:
            e = self.mem.setdefault(id(v.buf), {"w": [], "r": []})
            reg = (v.p0, v.p1, v.f0, v.f1)
            for (wreg, peng, pkind, pseq) in e["w"]:
                if ov(reg, wreg):
                    need(peng, pkind, pseq)
            if getattr(v.buf, "psum", False):
                for (rreg, peng, pkind, pseq) in e["r"]:
                    if peng != eng and ov(reg, rreg):
                        need(peng, pkind, pseq)
            e["r"].append((reg, eng, kind, seq))
            if len(e["r"]) > 512:
                e["r"] = e["r"][-256:]
        for v in writes:
            e = self.mem.setdefault(id(v.buf), {"w": [], "r": []})
            reg = (v.p0, v.p1, v.f0, v.f1)
            for (rreg, peng, pkind, pseq) in e["r"]:
                if (peng != eng or pkind == "d" or kind == "d") and ov(reg, rreg):
                    need(peng, pkind, pseq)
            for (wreg, peng, pkind, pseq) in e["w"]:
                if (peng != eng or pkind == "d" or kind == "d") and ov(reg, wreg):
                    need(peng, pkind, pseq)
            e["w"] = [w for w in e["w"]
                      if not (reg[0] <= w[0][0] and w[0][1] <= reg[1]
                              and reg[2] <= w[0][2] and w[0][3] <= reg[3])]
            e["w"].append((reg, eng, kind, seq))
            if len(e["w"]) > 512:
                e["w"] = e["w"][-256:]
        for (peng, pkind), pseq in op.waits.items():
            if pkind == "c":
                self.ops[peng][pseq].inc = True
            if peng != eng:
                self.clock[eng][(peng, pkind)] = max(
                    self.clock[eng].get((peng, pkind), -1), pseq)
        ops.append(op)
        return op

    def mm(self, out, lhsT, rhs, start, stop, tag="", tile_position=None):
        return self._emit(
            "pe", lambda e: e.matmul(out.ap, lhsT.ap, rhs.ap, start=start,
                                     stop=stop, tile_position=tile_position),
            [lhsT, rhs], [out], tag=tag)

    def transpose(self, out, in_, ident, tag=""):
        k = in_.p1 - in_.p0
        p0 = in_.p0
        if isinstance(ident, Buf):
            ident = ident.f(0, k, p0, p0 + k)
        elif (ident.p1 - ident.p0) != k or ident.p0 != p0:
            ident = ident.buf.f(0, k, p0, p0 + k)
        return self._emit(
            "pe", lambda e: e.transpose(out.ap, in_.ap, ident.ap),
            [in_, ident], [out], tag=tag)

    def tt(self, eng, out, a, b, op, tag=""):
        return self._emit(
            eng, lambda e: e.tensor_tensor(out.ap, a.ap, b.ap, op),
            [a, b], [out], tag=tag)

    def ts(self, eng, out, a, s1, s2, op0, op1, accum=None, tag=""):
        reads = [a]
        s1v = s1.ap if isinstance(s1, View) else s1
        s2v = s2.ap if isinstance(s2, View) else s2
        if isinstance(s1, View):
            reads.append(s1)
        if isinstance(s2, View):
            reads.append(s2)
        writes = [out] if accum is None else [out, accum]
        return self._emit(
            eng, lambda e: e.tensor_scalar(
                out.ap, a.ap, s1v, s2v, op0, op1,
                accum_out=None if accum is None else accum.ap),
            reads, writes, tag=tag)

    def tss(self, eng, out, a, s, op, tag=""):
        reads = [a] + ([s] if isinstance(s, View) else [])
        sv = s.ap if isinstance(s, View) else s
        return self._emit(
            eng, lambda e: e.tensor_single_scalar(out.ap, a.ap, sv, op),
            reads, [out], tag=tag)

    def stt(self, eng, out, in0, s, in1, op0, op1, tag=""):
        reads = [in0, in1] + ([s] if isinstance(s, View) else [])
        sv = s.ap if isinstance(s, View) else s
        return self._emit(
            eng, lambda e: e.scalar_tensor_tensor(out.ap, in0.ap, sv, in1.ap,
                                                  op0, op1),
            reads, [out], tag=tag)

    def scan(self, eng, out, d0, d1, init, op0, op1, tag=""):
        return self._emit(
            eng, lambda e: e.tensor_tensor_scan(out.ap, d0.ap, d1.ap, init,
                                                op0, op1),
            [d0, d1], [out], tag=tag)

    def red(self, eng, out, in_, op, negate=False, tag=""):
        return self._emit(
            eng, lambda e: e.tensor_reduce(out.ap, in_.ap, AX.X, op,
                                           negate=negate),
            [in_], [out], tag=tag)

    def copy(self, eng, out, in_, tag=""):
        return self._emit(
            eng, lambda e: e.tensor_copy(out.ap, in_.ap), [in_], [out], tag=tag)

    def memset(self, eng, out, val, tag=""):
        return self._emit(eng, lambda e: e.memset(out.ap, val), [], [out],
                          tag=tag)

    def recip(self, out, in_, tag=""):
        return self._emit("dve", lambda e: e.reciprocal(out.ap, in_.ap),
                          [in_], [out], tag=tag)

    def act(self, out, in_, func, bias=0.0, scale=1.0, accum=None, tag=""):
        reads = [in_]
        bv = bias.ap if isinstance(bias, View) else bias
        sv = scale.ap if isinstance(scale, View) else scale
        if isinstance(bias, View):
            reads.append(bias)
        if isinstance(scale, View):
            reads.append(scale)
        writes = [out] if accum is None else [out, accum]
        return self._emit(
            "act", lambda e: e.activation(
                out.ap, in_.ap, func, bias=bv, scale=sv,
                accum_out=None if accum is None else accum.ap),
            reads, writes, tag=tag)

    def iota(self, out, pattern, base=0, channel_multiplier=0, tag=""):
        return self._emit(
            "pool", lambda e: e.iota(out.ap, pattern, base=base,
                                     channel_multiplier=channel_multiplier,
                                     allow_small_or_imprecise_dtypes=True),
            [], [out], tag=tag)

    def dma(self, eng, out, in_, tag=""):
        return self._emit(
            eng, lambda e: e.dma_start(out=out.ap, in_=in_.ap),
            [in_], [out], kind="d", tag=tag)

    def build(self):
        nc = self.nc
        with self.stack:
            stack = self.stack
            csem, dsem = {}, {}
            for e in ENGINES:
                if any(o.kind == "c" and o.inc for o in self.ops[e]):
                    csem[e] = stack.enter_context(nc.semaphore("c_" + e))
                if self.ndma[e]:
                    dsem[e] = stack.enter_context(nc.semaphore("d_" + e))
            cpre, dpre = {}, {}
            for e in ENGINES:
                cc = dd = 0
                cl, dl = [], []
                for o in self.ops[e]:
                    if o.kind == "c" and o.inc:
                        cc += 1
                    if o.kind == "d":
                        dd += 1
                    cl.append(cc)
                    dl.append(dd)
                cpre[e], dpre[e] = cl, dl

            with nc.Block() as block:
                for e in ENGINES:
                    if not self.ops[e]:
                        continue

                    def mkclosure(e=e):
                        def closure(eng):
                            for o in self.ops[e]:
                                for (peng, pkind), pseq in sorted(o.waits.items()):
                                    if pkind == "c":
                                        eng.wait_ge(csem[peng],
                                                    cpre[peng][pseq])
                                    else:
                                        eng.wait_ge(dsem[peng],
                                                    16 * dpre[peng][pseq])
                                inst = o.fn(eng)
                                if o.kind == "d":
                                    inst.then_inc(dsem[e], 16)
                                elif o.inc:
                                    inst.then_inc(csem[e], 1)
                        return closure

                    getattr(block, _BLOCK_METHOD[e])(mkclosure())

                def final(eng):
                    for e in ENGINES:
                        if e in dsem:
                            eng.wait_ge(dsem[e], 16 * self.ndma[e])
                        if e in csem:
                            eng.wait_ge(csem[e], cpre[e][-1])
                block.sync(final)
        return nc



def setup_common(kb):
    cm = {}
    cm["identf"] = kb.sbuf("identf", 128, 128, F32)
    cm["ident"] = kb.sbuf("ident", 128, 128, BF16)
    cm["identf32"] = kb.sbuf("identf32", 128, 128, F32)
    cm["ones_r"] = kb.sbuf("ones_r", 1, 128, BF16)
    cm["ones_c"] = kb.sbuf("ones_c", 128, 1, BF16)
    kb.iota(cm["identf"].f(), [[1, 128]], channel_multiplier=-1)
    kb.tss("dve", cm["ident"].f(), cm["identf"].f(), 0.0, ALU.is_equal)
    kb.tss("dve", cm["identf32"].f(), cm["identf"].f(), 0.0, ALU.is_equal)
    kb.memset("dve", cm["ones_r"].f(), 1.0)
    kb.memset("dve", cm["ones_c"].f(), 1.0)
    kb.memset("dve", cm["ones_cf"].f(), 1.0)
    return cm


def load_w_fm(kb, dram, wbuf, n_in=D, n_out=D, eng="pool", woff=0):
    nci = n_in // 128
    src = dram.base().rearrange("(c p) n -> p c n", p=128)
    dstap = wbuf.base()[:, woff:woff + nci * n_out].rearrange(
        "p (c n) -> p c n", c=nci)
    kb.dma(eng, wbuf.wv(dstap, 0, 128, woff, woff + nci * n_out),
           dram.wv(src), tag="load_w")


def load_w_T(kb, dram, raw, wT, cm, pt, n_rows=D, n_cols=D):
    ndj = n_rows // 128
    nci = n_cols // 128
    load_w_fm(kb, dram, raw, n_in=n_rows, n_out=n_cols)
    for ci in range(nci):
        for dj in range(ndj):
            kb.transpose(pt.f(dj * 128, dj * 128 + 128),
                         raw.c(dj, n_cols, ci * 128, ci * 128 + 128),
                         cm["ident"], tag="wT_tr")
        kb.copy("dve", wT.f(ci * n_rows, ci * n_rows + ndj * 128),
                pt.f(0, ndj * 128), tag="wT_ev")


def emit_vec_pp(kb, dram, pp, rowbuf, cm, ptf, dlen=D, col0=0):
    n = dlen // 128
    flat = dram.base()
    if flat.ndim > 1:
        pat = " ".join(chr(97 + i) for i in range(flat.ndim))
        flat = flat.rearrange(f"{pat} -> ({pat})")
    src = flat.rearrange("(o d) -> o d", o=1)
    kb.dma("sp", rowbuf.f(0, dlen, 0, 1), dram.wv(src), tag="vec_ld")
    for i in range(n):
        kb.transpose(ptf.f(i, i + 1), rowbuf.f(i * 128, (i + 1) * 128, 0, 1),
                     cm["identf32"], tag="vec_tr")
    kb.copy("dve", pp.f(col0, col0 + n), ptf.f(0, n), tag="vec_ev")





def setup_common(kb):
    cm = {}
    cm["identf"] = kb.sbuf("identf", 128, 128, F32)
    cm["ident"] = kb.sbuf("ident", 128, 128, BF16)
    cm["ones_r"] = kb.sbuf("ones_r", 1, 128, BF16)
    cm["ones_rf"] = kb.sbuf("ones_rf", 1, 128, F32)
    cm["ones_c"] = kb.sbuf("ones_c", 128, 1, BF16)
    cm["ones_cf"] = kb.sbuf("ones_cf", 128, 1, F32)
    kb.iota(cm["identf"].f(), [[1, 128]], channel_multiplier=-1)
    kb.tss("dve", cm["ident"].f(), cm["identf"].f(), 0.0, ALU.is_equal)
    kb.tss("dve", cm["identf"].f(), cm["identf"].f(), 0.0, ALU.is_equal)
    cm["identf32"] = cm["identf"]
    cm["ident64"] = kb.sbuf("ident64", 128, 64, BF16)
    kb.iota(cm["ident64"].f(), [[1, 64]], base=64, channel_multiplier=-1)
    kb.tss("dve", cm["ident64"].f(), cm["ident64"].f(), 0.0, ALU.is_equal)
    kb.memset("dve", cm["ones_r"].f(), 1.0)
    kb.memset("dve", cm["ones_rf"].f(), 1.0)
    kb.memset("dve", cm["ones_c"].f(), 1.0)
    kb.memset("dve", cm["ones_cf"].f(), 1.0)
    return cm


def make_bufs(kb):
    bs = {}
    bs["X"] = kb.sbuf("X", 128, NCH * T, F32)
    bs["XN"] = kb.sbuf("XN", 128, NCH * T, BF16)
    bs["ACC"] = kb.sbuf("ACC", 128, NCH * T, BF16)
    bs["PBF"] = kb.sbuf("PBF", 128, NCH * T, BF16)
    bs["PF32"] = kb.sbuf("PF32", 128, 4 * T, F32)
    bs["WS"] = kb.sbuf("WS", 128, 3072, BF16)
    bs["HB2"] = kb.sbuf("HB2", 128, 2048, BF16)
    bs["PP"] = kb.sbuf("PP", 128, 80, F32)
    bs["BPP"] = kb.sbuf("BPP", 128, 320, F32)
    bs["PM0"] = kb.psumb("PM0", 128, 512, F32)
    bs["PM1"] = kb.psumb("PM1", 128, 512, F32)
    bs["PS0"] = kb.psumb("PS0", 128, 512, F32)
    bs["PS1"] = kb.psumb("PS1", 128, 512, F32)
    bs["PO"] = kb.psumb("PO", 128, 512, F32)
    bs["PB"] = kb.psumb("PB", 128, 512, F32)
    bs["PR"] = kb.psumb("PR", 128, 512, F32)
    bs["PT"] = kb.psumb("PT", 128, 1024, BF16)
    return bs


def TA(bs):
    return bs["PF32"], 0


def _pfv(bs, base, f0, f1, p0=0, p1=None):
    return bs["PF32"].f(base + f0, base + f1, p0, p1)


def pfTA(bs, f0=0, f1=T, p0=0, p1=None):
    return _pfv(bs, 0, f0, f1, p0, p1)


def pfTB(bs, f0=0, f1=T, p0=0, p1=None):
    return _pfv(bs, T, f0, f1, p0, p1)


def pfTC(bs, f0=0, f1=T, p0=0, p1=None):
    return _pfv(bs, 2 * T, f0, f1, p0, p1)


def pfTD(bs, f0=0, f1=T, p0=0, p1=None):
    return _pfv(bs, 3 * T, f0, f1, p0, p1)


def emit_vec_pp(kb, bs, cm, dram, col0, dlen=D, scale=None):
    n = dlen // 128
    flat = dram.base()
    if flat.ndim > 1:
        pat = " ".join(chr(97 + i) for i in range(flat.ndim))
        flat = flat.rearrange(f"{pat} -> ({pat})")
    src = flat.rearrange("(o d) -> o d", o=1)
    row = pfTD(bs, 0, dlen, 0, 1)
    kb.dma("sp", row, dram.wv(src), tag="vec_ld")
    PR = bs["PR"]
    for i in range(n):
        kb.transpose(PR.f(i, i + 1), pfTD(bs, i * 128, (i + 1) * 128, 0, 1),
                     cm["identf32"], tag="vec_tr")
    kb.copy("dve", bs["PP"].f(col0, col0 + n), PR.f(0, n), tag="vec_ev")


def emit_mat_pp(kb, bs, cm, dram, dest, col0, R, C):
    nj = C // 128
    stage = pfTD(bs, 0, C, 0, R)
    kb.dma("sp", stage, dram.whole(), tag="matpp_ld")
    PR = bs["PR"]
    for j in range(nj):
        kb.transpose(PR.f(j * R, (j + 1) * R),
                     pfTD(bs, j * 128, (j + 1) * 128, 0, R),
                     cm["identf32"], tag="matpp_tr")
        if (j + 1) * R > 512:
            raise RuntimeError("matpp overflow")
    kb.copy("dve", dest.f(col0, col0 + nj * R), PR.f(0, nj * R), tag="matpp_ev")


def emit_load_x_fm(kb, bs, cm, xdram, X):
    xtm0 = lambda: pfTA(bs, 0, 1024)
    xtm1 = lambda: pfTB(bs, 0, 1024)
    PS0, PS1 = bs["PS0"], bs["PS1"]
    for ti in range(NTT):
        xtm = xtm0() if ti % 2 == 0 else xtm1()
        src = xdram.base()[ti * 128:(ti + 1) * 128, :]
        kb.dma("sp", xtm, xdram.wv(src, ti * 128 * D, (ti + 1) * 128 * D),
               tag="x_ld")
        for ch in range(NCH):
            ps = PS0 if ch % 2 == 0 else PS1
            base = 0 if ti % 2 == 0 else T
            kb.transpose(ps.f(0, 128),
                         _pfv(bs, base, ch * 128, (ch + 1) * 128),
                         cm["identf32"], tag="x_tr")
            kb.copy("dve", X.t(ch, ti * 128, (ti + 1) * 128), ps.f(0, 128),
                    tag="x_ev")


def emit_store_x(kb, bs, cm, X, xodram):
    PS0, PS1 = bs["PS0"], bs["PS1"]
    for ti in range(NTT):
        base = 0 if ti % 2 == 0 else T
        for half in range(2):
            ps = PS0 if half == 0 else PS1
            for c4 in range(4):
                ch = half * 4 + c4
                kb.transpose(ps.f(c4 * 128, (c4 + 1) * 128),
                             X.t(ch, ti * 128, (ti + 1) * 128),
                             cm["identf32"], tag="xo_tr")
            kb.copy("dve", _pfv(bs, base, half * 512, (half + 1) * 512),
                    ps.f(0, 512), tag="xo_ev")
        dst = xodram.base()[ti * 128:(ti + 1) * 128, :]
        kb.dma("sp", xodram.wv(dst, ti * 128 * D, (ti + 1) * 128 * D),
               _pfv(bs, base, 0, 1024), tag="x_st")


def emit_ln(kb, bs, cm, X, XN):
    PRs, PB0 = bs["PR"], bs["PB"]
    onec = cm["ones_c"]
    SQ = bs["PBF"]
    RA = lambda f0, f1: pfTA(bs, f0, f1, 0, 1)
    RB = lambda f0, f1: pfTB(bs, f0, f1, 0, 1)
    RC = lambda f0, f1: pfTC(bs, f0, f1, 0, 1)
    for tc in range(4):
        t0, t1 = tc * 512, (tc + 1) * 512
        for ch in range(NCH):
            kb.act(SQ.f(0, 512), X.t(ch, t0, t1), AF.Square, tag="ln_sq")
            kb.mm(PRs.f(0, 512, 0, 1), cm["ones_cf"].f(), X.t(ch, t0, t1),
                  start=(ch == 0), stop=(ch == NCH - 1), tag="ln_sum")
            kb.mm(PB0.f(0, 512, 0, 1), onec.f(), SQ.f(0, 512),
                  start=(ch == 0), stop=(ch == NCH - 1), tag="ln_sumsq")
        kb.ts("dve", RA(t0, t1), PRs.f(0, 512, 0, 1), 1.0 / D, None,
              ALU.mult, ALU.bypass, tag="ln_mean")
        kb.ts("dve", RB(t0, t1), PB0.f(0, 512, 0, 1), 1.0 / D, None,
              ALU.mult, ALU.bypass, tag="ln_msq")
    kb.tt("dve", RC(0, T), RA(0, T), RA(0, T), ALU.mult, tag="ln_m2")
    kb.tt("dve", RB(0, T), RB(0, T), RC(0, T), ALU.subtract, tag="ln_var")
    kb.ts("dve", RB(0, T), RB(0, T), 1e-5, None, ALU.add, ALU.bypass,
          tag="ln_eps")
    kb.act(RC(0, T), RB(0, T), AF.Sqrt, tag="ln_sqrt")
    kb.recip(RB(0, T), RC(0, T), tag="ln_recip")
    kb.tt("dve", RC(0, T), RA(0, T), RB(0, T), ALU.mult, tag="ln_mr")
    kb.ts("dve", RC(0, T), RC(0, T), -1.0, None, ALU.mult, ALU.bypass,
          tag="ln_neg")
    for tc in range(4):
        t0, t1 = tc * 512, (tc + 1) * 512
        kb.mm(PRs.f(), cm["ones_rf"].f(), RB(t0, t1),
              start=True, stop=True, tag="ln_bc_r")
        kb.mm(PB0.f(), cm["ones_rf"].f(), RC(t0, t1),
              start=True, stop=True, tag="ln_bc_m")
        for ch in range(NCH):
            kb.tt("dve", pfTC(bs, 0, 512), X.t(ch, t0, t1), PRs.f(),
                  ALU.mult, tag="ln_ap1")
            kb.tt("dve", XN.t(ch, t0, t1), pfTC(bs, 0, 512), PB0.f(),
                  ALU.add, tag="ln_ap2")



def load_w_slice(kb, dram, ws, slot_off, dj, n_in=D, eng="pool", colw=128,
                 row0=0, n_rows=None):
    if n_rows is None:
        n_rows = n_in
    nci = n_in // 128
    src = dram.base()[row0:row0 + n_rows, dj * colw:(dj + 1) * colw]
    src = src.rearrange("(c p) n -> p c n", p=128)
    dst = ws.base()[:, slot_off:slot_off + nci * colw].rearrange(
        "p (c n) -> p c n", c=nci)
    kb.dma(eng, ws.wv(dst, 0, 128, slot_off, slot_off + nci * colw),
           dram.wv(src), tag="w_slice")


def emit_l0(kb, bs, cm, Wd, ato_dram):
    X, XN, ACC, PBF = bs["X"], bs["XN"], bs["ACC"], bs["PBF"]
    WS, HB2, PP, BPP = bs["WS"], bs["HB2"], bs["PP"], bs["BPP"]
    PM0, PM1, PS0, PS1 = bs["PM0"], bs["PM1"], bs["PS0"], bs["PS1"]
    PO, PB, PR, PT = bs["PO"], bs["PB"], bs["PR"], bs["PT"]
    onesr, onesrf, onesc, onescf = (cm["ones_r"], cm["ones_rf"], cm["ones_c"],
                                    cm["ones_cf"])
    E0, I0 = 8, 4096

    emit_ln(kb, bs, cm, X, XN)
    for i, nm in enumerate(["bq", "bk", "bv", "bo", "gn_w", "gn_b"]):
        emit_vec_pp(kb, bs, cm, Wd[nm], i * 8)
    for half in range(2):
        src = Wd["b1"].base()[:, half * 2048:(half + 1) * 2048]
        stage = pfTD(bs, 0, 2048, 0, 8)
        kb.dma("sp", stage, Wd["b1"].wv(src), tag="b1_ld")
        for j in range(16):
            kb.transpose(PR.f(j * 8, (j + 1) * 8),
                         pfTD(bs, j * 128, (j + 1) * 128, 0, 8),
                         cm["identf32"], tag="b1_tr")
        for j in range(16):
            for e in range(E0):
                kb.copy("dve", BPP.f(e * 32 + half * 16 + j,
                                     e * 32 + half * 16 + j + 1),
                        PR.f(j * 8 + e, j * 8 + e + 1), tag="b1_ev")
    stage = pfTD(bs, 0, 1024, 0, 8)
    kb.dma("sp", stage, Wd["b2"].whole(), tag="b2_ld")
    for j in range(8):
        kb.transpose(PR.f(j * 8, (j + 1) * 8),
                     pfTD(bs, j * 128, (j + 1) * 128, 0, 8),
                     cm["identf32"], tag="b2_tr")
    for j in range(8):
        for e in range(E0):
            kb.copy("dve", BPP.f(256 + e * 8 + j, 256 + e * 8 + j + 1),
                    PR.f(j * 8 + e, j * 8 + e + 1), tag="b2_ev")

    for ch in range(NCH):
        kb.memset("pool", ACC.t(ch), 0.0, tag="acc_z")
    load_w_fm(kb, Wd["wr"], WS, n_in=D, n_out=E0, eng="pool", woff=2048)
    kb.dma("pool", WS.f(2112, 2120, 0, 1), Wd["br"].wv(
        Wd["br"].base().rearrange("(o d) -> o d", o=1)), tag="br_ld")
    for tt in range(NTT):
        t0, t1 = tt * 128, (tt + 1) * 128
        for ci in range(NCH):
            kb.mm(PM0.f(0, E0), XN.t(ci, t0, t1),
                  WS.f(2048 + ci * E0, 2048 + (ci + 1) * E0),
                  start=(ci == 0), stop=False, tag="rt_mm")
        kb.mm(PM0.f(0, E0), onesr.f(), WS.f(2112, 2120, 0, 1),
              start=False, stop=True, tag="rt_bias")
        GE = pfTB(bs, 0, E0)
        M1 = pfTB(bs, 16, 17)
        M2 = pfTB(bs, 17, 18)
        SU = pfTB(bs, 18, 19)
        kb.red("dve", pfTB(bs, 19, 20), PM0.f(0, E0), ALU.max, negate=True,
               tag="rt_nmx")
        kb.act(GE, PM0.f(0, E0), AF.Exp, bias=pfTB(bs, 19, 20), tag="rt_exp")
        kb.red("dve", M1, GE, ALU.max, tag="rt_m1")
        MK = pfTB(bs, 8, 16)
        kb.ts("dve", MK, GE, M1, None, ALU.is_ge, ALU.bypass, tag="rt_mk")
        E2 = pfTB(bs, 20, 28)
        kb.ts("dve", E2, MK, -1.0, 1.0, ALU.mult, ALU.add, tag="rt_im")
        kb.tt("dve", E2, GE, E2, ALU.mult, tag="rt_e2")
        kb.red("dve", M2, E2, ALU.max, tag="rt_m2")
        kb.tt("dve", SU, M1, M2, ALU.add, tag="rt_su")
        kb.recip(SU, SU, tag="rt_rc")
        kb.tt("dve", M1, M1, SU, ALU.mult, tag="rt_gate")
        kb.ts("dve", MK, MK, M1, None, ALU.mult, ALU.bypass, tag="rt_ge")
        kb.transpose(PR.f(0, 128, 0, E0), MK, cm["identf32"], tag="rt_tr")
        kb.copy("dve", pfTD(bs, t0, t1, 0, E0), PR.f(0, 128, 0, E0),
                tag="rt_gev")

    for e in range(E0):
        kb.dma("sp", pfTC(bs, 0, T, 0, 1), pfTD(bs, 0, T, e, e + 1),
               tag="ge_row")
        kb.copy("dve", HB2.f(0, T, 0, 1), pfTC(bs, 0, T, 0, 1), tag="ge_bf")
        kb.dma("pool", WS.f(2048, 3072, 0, 1),
               Wd["b2"].wv(Wd["b2"].base()[e:e + 1, :], e * 1024,
                           (e + 1) * 1024), tag="b2row")
        for tc in range(4):
            t0, t1 = tc * 512, (tc + 1) * 512
            kb.mm(PB.f(0, 512), onesrf.f(), pfTC(bs, t0, t1, 0, 1),
                  start=True, stop=True, tag="ge_bc")
            GBC = pfTB(bs, 0, 512)
            kb.copy("dve", GBC, PB.f(0, 512), tag="ge_bcc")
            for it in range(32):
                slot = 0 if it % 2 == 0 else 1024
                load_w_slice(kb, Wd["w1"], WS, slot, it, n_in=D, eng="pool",
                             colw=128, row0=e * D, n_rows=D)
                pm = PM0 if it % 2 == 0 else PM1
                for ci in range(NCH):
                    kb.mm(pm.f(0, 512), WS.f(slot + ci * 128, slot + (ci + 1) * 128),
                          XN.t(ci, t0, t1), start=(ci == 0), stop=(ci == NCH - 1),
                          tag="h_mm")
                kb.act(PBF.f(it * 512, (it + 1) * 512), pm.f(0, 512), AF.Gelu,
                       bias=BPP.f(e * 32 + it, e * 32 + it + 1), tag="h_gelu")
                kb.tt("dve", PBF.f(it * 512, (it + 1) * 512),
                      PBF.f(it * 512, (it + 1) * 512), GBC, ALU.mult,
                      tag="h_gate")
            for grp in range(2):
                banks = [PM0, PM1, PS0, PS1]
                for it in range(32):
                    slot = 0 if it % 2 == 0 else 1024
                    src = Wd["w2"].base()[e * I0 + it * 128:
                                          e * I0 + (it + 1) * 128,
                                          grp * 512:(grp + 1) * 512]
                    kb.dma("pool", WS.wv(WS.base()[:, slot:slot + 512], 0, 128,
                                         slot, slot + 512),
                           Wd["w2"].wv(src), tag="w2_slice")
                    for cq in range(4):
                        kb.mm(banks[cq].f(0, 512),
                              WS.f(slot + cq * 128, slot + (cq + 1) * 128),
                              PBF.f(it * 512, (it + 1) * 512),
                              start=(it == 0), stop=False, tag="d_mm")
                for cq in range(4):
                    co = grp * 4 + cq
                    kb.mm(banks[cq].f(0, 512),
                          WS.f(2048 + co * 128, 2048 + (co + 1) * 128, 0, 1),
                          HB2.f(t0, t1, 0, 1),
                          start=False, stop=True, tag="d_b2")
                for cq in range(4):
                    co = grp * 4 + cq
                    kb.tt("dve", ACC.c(co, T, t0, t1), ACC.c(co, T, t0, t1),
                          banks[cq].f(0, 512), ALU.add, tag="d_acc")
    emit_l0_attn(kb, bs, cm, Wd, ato_dram)


def emit_l0_attn(kb, bs, cm, Wd, ato):
    X, XN, ACC, PBF = bs["X"], bs["XN"], bs["ACC"], bs["PBF"]
    WS, HB2, PP = bs["WS"], bs["HB2"], bs["PP"]
    PM0, PM1, PS0, PS1 = bs["PM0"], bs["PM1"], bs["PS0"], bs["PS1"]
    PO, PB, PR, PT = bs["PO"], bs["PB"], bs["PR"], bs["PT"]
    onesr, onesrf, onesc = cm["ones_r"], cm["ones_rf"], cm["ones_c"]
    SCALE = 0.125

    for i, nm in enumerate(["lq1", "lk1", "lq2", "lk2"]):
        kb.dma("sp", pfTD(bs, i * 64, (i + 1) * 64, 0, 16), Wd[nm].whole(),
               tag="lam_ld")
    kb.tt("dve", pfTD(bs, 256, 320, 0, 16), pfTD(bs, 0, 64, 0, 16),
          pfTD(bs, 64, 128, 0, 16), ALU.mult, tag="lam_p1")
    kb.red("dve", pfTD(bs, 320, 321, 0, 16), pfTD(bs, 256, 320, 0, 16),
           ALU.add, tag="lam_s1")
    kb.tt("dve", pfTD(bs, 256, 320, 0, 16), pfTD(bs, 128, 192, 0, 16),
          pfTD(bs, 192, 256, 0, 16), ALU.mult, tag="lam_p2")
    kb.red("dve", pfTD(bs, 321, 322, 0, 16), pfTD(bs, 256, 320, 0, 16),
           ALU.add, tag="lam_s2")
    kb.act(pfTD(bs, 322, 323, 0, 16), pfTD(bs, 320, 321, 0, 16), AF.Exp,
           tag="lam_e1")
    kb.act(pfTD(bs, 323, 324, 0, 16), pfTD(bs, 321, 322, 0, 16), AF.Exp,
           tag="lam_e2")
    kb.tt("dve", pfTD(bs, 324, 325, 0, 16), pfTD(bs, 322, 323, 0, 16),
          pfTD(bs, 323, 324, 0, 16), ALU.subtract, tag="lam_d")
    kb.ts("dve", pfTD(bs, 325, 326, 0, 16), pfTD(bs, 324, 325, 0, 16),
          0.8, SCALE, ALU.add, ALU.mult, tag="lam_sc")
    kb.transpose(PR.f(0, 16, 0, 1), pfTD(bs, 325, 326, 0, 16),
                 cm["identf32"], tag="lam_tr")
    kb.copy("dve", pfTC(bs, 32, 48, 0, 1), PR.f(0, 16, 0, 1), tag="lam_row")
    kb.mm(PB.f(0, 16), onesrf.f(), pfTC(bs, 32, 48, 0, 1), start=True,
          stop=True, tag="lam_bc")
    kb.copy("dve", PP.f(48, 64), PB.f(0, 16), tag="lam_ev")

    QO, KO, VO = 0, 2048, 4096
    EXO, VTO = 6144, 14336
    for dj in range(NCH):
        for w_i, (wnm, bcol, dsto) in enumerate(
                [("Wq", 0, QO), ("Wk", 8, KO), ("Wv", 16, VO)]):
            slot = 1024 * (w_i % 2)
            load_w_slice(kb, Wd[wnm], WS, slot, dj, eng="pool")
            for tc in range(4):
                t0, t1 = tc * 512, (tc + 1) * 512
                pm = PM0 if tc % 2 == 0 else PM1
                for ci in range(NCH):
                    kb.mm(pm.f(0, 512), WS.f(slot + ci * 128,
                                             slot + (ci + 1) * 128),
                          XN.t(ci, t0, t1), start=(ci == 0),
                          stop=(ci == NCH - 1), tag="qkv_mm")
                kb.ts("dve", PBF.f(dsto + t0, dsto + t1), pm.f(0, 512),
                      PP.f(bcol + dj, bcol + dj + 1), None, ALU.add,
                      ALU.bypass, tag="qkv_b")
        for half in range(2):
            h = 2 * dj + half
            p0, p1 = half * 64, half * 64 + 64
            kb.ts("dve", PBF.f(QO, QO + T, p0, p1), PBF.f(QO, QO + T, p0, p1),
                  PP.f(48 + h, 49 + h, p0, p1), None, ALU.mult, ALU.bypass,
                  tag="q_lam")
            kb.memset("pool", PBF.f(VTO, VTO + 1040), 1.0, tag="vta_one")
            idv = cm["ident"] if half == 0 else cm["ident64"]
            for jt in range(NTT):
                kb.transpose(PT.f(jt * 64, (jt + 1) * 64),
                             PBF.f(VO + jt * 128, VO + (jt + 1) * 128, p0, p1),
                             idv, tag="vta_tr")
            for jt in range(NTT):
                kb.copy("dve", PBF.f(VTO + jt * 65, VTO + jt * 65 + 64),
                        PT.f(jt * 64, (jt + 1) * 64), tag="vta_ev")
            kb.act(PBF.f(EXO, EXO + T, p0, p1), PBF.f(KO, KO + T, p0, p1),
                   AF.Square, tag="kn_sq")
            for tc in range(4):
                kb.mm(PR.f(0, 512, 0, 1), onesc.f(0, 1, p0, p1),
                      PBF.f(EXO + tc * 512, EXO + (tc + 1) * 512, p0, p1),
                      start=True, stop=True, tag="kn_mm")
                kb.copy("dve", pfTB(bs, tc * 512, (tc + 1) * 512, 0, 1),
                        PR.f(0, 512, 0, 1), tag="kn_ev")
            kb.red("dve", pfTB(bs, 2040, 2041, 0, 1), pfTB(bs, 0, T, 0, 1),
                   ALU.max, tag="kn_max")
            kb.act(pfTB(bs, 2041, 2042, 0, 1), pfTB(bs, 2040, 2041, 0, 1),
                   AF.Sqrt, tag="kn_sqrt")
            kb.act(PBF.f(EXO, EXO + T, p0, p1), PBF.f(QO, QO + T, p0, p1),
                   AF.Square, tag="qn_sq")
            for tc in range(4):
                kb.mm(PR.f(0, 512, 0, 1), onesc.f(0, 1, p0, p1),
                      PBF.f(EXO + tc * 512, EXO + (tc + 1) * 512, p0, p1),
                      start=True, stop=True, tag="qn_mm")
                kb.copy("dve", pfTA(bs, tc * 512, (tc + 1) * 512, 0, 1),
                        PR.f(0, 512, 0, 1), tag="qn_ev")
            kb.act(pfTA(bs, 0, T, 0, 1), pfTA(bs, 0, T, 0, 1), AF.Sqrt,
                   tag="qn_sqrt")
            kb.tss("dve", pfTA(bs, 0, T, 0, 1), pfTA(bs, 0, T, 0, 1),
                   pfTB(bs, 2041, 2042, 0, 1), ALU.mult, tag="qn_k")
            kb.ts("dve", HB2.f(0, T, 0, 1), pfTA(bs, 0, T, 0, 1), -1.0, None,
                  ALU.mult, ALU.bypass, tag="qn_nm")
            for IT in range(4):
                i0, i1 = IT * 512, (IT + 1) * 512
                for jt in range(NTT):
                    ps = PS0 if jt % 2 == 0 else PS1
                    kb.mm(ps.f(0, 512), PBF.f(KO + jt * 128,
                                              KO + (jt + 1) * 128, p0, p1),
                          PBF.f(QO + i0, QO + i1, p0, p1), start=True,
                          stop=False, tag="sc_mm")
                    kb.mm(ps.f(0, 512), onesr.f(), HB2.f(i0, i1, 0, 1),
                          start=False, stop=True, tag="sc_nm")
                    kb.act(PBF.f(EXO + jt * 512, EXO + (jt + 1) * 512),
                           ps.f(0, 512), AF.Exp, tag="sc_exp")
                    kb.mm(PO.f(0, 512, 0, 65),
                          PBF.f(VTO + jt * 65, VTO + (jt + 1) * 65),
                          PBF.f(EXO + jt * 512, EXO + (jt + 1) * 512),
                          start=(jt == 0), stop=(jt == NTT - 1), tag="av_mm")
                kb.recip(pfTB(bs, 0, 512, 0, 1), PO.f(0, 512, 64, 65),
                         tag="av_rc")
                kb.mm(PB.f(0, 512, 0, 64), onesrf.f(0, 64, 0, 1),
                      pfTB(bs, 0, 512, 0, 1), start=True, stop=True,
                      tag="av_bc")
                kb.copy("dve", pfTC(bs, 512, 1024, 0, 64), PB.f(0, 512, 0, 64),
                        tag="av_bcc")
                kb.tt("dve", WS.f(2048, 2560, 0, 64), PO.f(0, 512, 0, 64),
                      pfTC(bs, 512, 1024, 0, 64), ALU.mult, tag="av_out")
                kb.mm(PR.f(0, 512, 0, 1), onesc.f(0, 1, 0, 64),
                      WS.f(2048, 2560, 0, 64), start=(IT == 0),
                      stop=(IT == 3), tag="gn_sum")
                kb.act(WS.f(2560, 3072, 0, 64), WS.f(2048, 2560, 0, 64),
                       AF.Square, tag="gn_sq")
                kb.mm(PM0.f(0, 512, 0, 1), onesc.f(0, 1, 0, 64),
                      WS.f(2560, 3072, 0, 64), start=(IT == 0),
                      stop=(IT == 3), tag="gn_sumsq")
                dst = ato.base()[h * 64:(h + 1) * 64, i0:i1]
                kb.dma("sp", ato.wv(dst), WS.f(2048, 2560, 0, 64),
                       tag="ato_st")
            NEL = 64.0 * T
            kb.red("dve", pfTB(bs, 600, 601, 0, 1), PR.f(0, 512, 0, 1),
                   ALU.add, tag="gn_s")
            kb.ts("dve", pfTC(bs, h, h + 1, 0, 1), pfTB(bs, 600, 601, 0, 1),
                  1.0 / NEL, None, ALU.mult, ALU.bypass, tag="gn_mean")
            kb.red("dve", pfTB(bs, 601, 602, 0, 1), PM0.f(0, 512, 0, 1),
                   ALU.add, tag="gn_ss")
            kb.ts("dve", pfTB(bs, 601, 602, 0, 1), pfTB(bs, 601, 602, 0, 1),
                  1.0 / NEL, None, ALU.mult, ALU.bypass, tag="gn_msq")
            kb.tt("dve", pfTB(bs, 602, 603, 0, 1), pfTC(bs, h, h + 1, 0, 1),
                  pfTC(bs, h, h + 1, 0, 1), ALU.mult, tag="gn_m2")
            kb.tt("dve", pfTB(bs, 601, 602, 0, 1), pfTB(bs, 601, 602, 0, 1),
                  pfTB(bs, 602, 603, 0, 1), ALU.subtract, tag="gn_var")
            kb.ts("dve", pfTB(bs, 601, 602, 0, 1), pfTB(bs, 601, 602, 0, 1),
                  1e-5, None, ALU.add, ALU.bypass, tag="gn_eps")
            kb.act(pfTB(bs, 601, 602, 0, 1), pfTB(bs, 601, 602, 0, 1),
                   AF.Sqrt, tag="gn_sd")
            kb.recip(pfTC(bs, 16 + h, 17 + h, 0, 1), pfTB(bs, 601, 602, 0, 1),
                     tag="gn_rstd")

    for dj in range(NCH):
        for half in range(2):
            h = 2 * dj + half
            p0, p1 = half * 64, half * 64 + 64
            tp = (0, 64) if half == 1 else None
            kb.mm(PB.f(0, 1, p0, p1), onesrf.f(0, 64, 0, 1),
                  pfTC(bs, 16 + h, 17 + h, 0, 1), start=True, stop=True,
                  tile_position=tp, tag="ab_rbc")
            kb.tt("dve", PP.f(48 + dj, 49 + dj, p0, p1),
                  PP.f(32 + dj, 33 + dj, p0, p1), PB.f(0, 1, p0, p1),
                  ALU.mult, tag="ab_a")
            kb.mm(PB.f(1, 2, p0, p1), onesrf.f(0, 64, 0, 1),
                  pfTC(bs, h, h + 1, 0, 1), start=True, stop=True,
                  tile_position=tp, tag="ab_mbc")
            kb.tt("dve", pfTB(bs, 610, 611, p0, p1),
                  PP.f(48 + dj, 49 + dj, p0, p1), PB.f(1, 2, p0, p1),
                  ALU.mult, tag="ab_ma")
            kb.tt("dve", PP.f(56 + dj, 57 + dj, p0, p1),
                  PP.f(40 + dj, 41 + dj, p0, p1), pfTB(bs, 610, 611, p0, p1),
                  ALU.subtract, tag="ab_b")

    for dj in range(NCH):
        load_w_slice(kb, Wd["Wo"], PBF_ws(bs, dj), 0, dj, eng="pool")
    for tc in range(4):
        t0, t1 = tc * 512, (tc + 1) * 512
        for ci in range(NCH):
            src = ato.base()[ci * 128:(ci + 1) * 128, t0:t1]
            kb.dma("sp", PBF.f(ci * 512, (ci + 1) * 512), ato.wv(src),
                   tag="ato_ld")
            kb.ts("dve", PBF.f(ci * 512, (ci + 1) * 512),
                  PBF.f(ci * 512, (ci + 1) * 512),
                  PP.f(48 + ci, 49 + ci), PP.f(56 + ci, 57 + ci),
                  ALU.mult, ALU.add, tag="ato_nrm")
        for dj in range(NCH):
            pm = PM0 if dj % 2 == 0 else PM1
            for ci in range(NCH):
                kb.mm(pm.f(0, 512),
                      PBF.f(4096 + dj * 1024 + ci * 128,
                            4096 + dj * 1024 + (ci + 1) * 128),
                      PBF.f(ci * 512, (ci + 1) * 512), start=(ci == 0),
                      stop=(ci == NCH - 1), tag="wo_mm")
            kb.ts("dve", pfTB(bs, 0, 512), pm.f(0, 512),
                  PP.f(24 + dj, 25 + dj), None, ALU.add, ALU.bypass,
                  tag="wo_bias")
            kb.act(pfTC(bs, 0, 512), ACC.t(dj, t0, t1), AF.Silu,
                   tag="res_silu")
            kb.tt("dve", pfTB(bs, 0, 512), pfTB(bs, 0, 512), pfTC(bs, 0, 512),
                  ALU.mult, tag="res_mul")
            kb.tt("dve", X.t(dj, t0, t1), X.t(dj, t0, t1), pfTB(bs, 0, 512),
                  ALU.add, tag="res_add")


def PBF_ws(bs, dj):
    class _F:
        def __init__(self, buf, off):
            self.buf, self.off = buf, off

        def wv(self, ap, p0, p1, f0, f1):
            return self.buf.wv(ap, p0, p1, self.off + f0, self.off + f1)

        @property
        def base(self):
            b = self.buf
            off = self.off
            return lambda: b.base()[:, off:off + 1024]
    return _F(bs["PBF"], 4096 + dj * 1024)



L0_SHAPES = {
    "Wq": [D, D], "bq": [D], "Wk": [D, D], "bk": [D], "Wv": [D, D],
    "bv": [D], "Wo": [D, D], "bo": [D], "lq1": [16, 64], "lk1": [16, 64],
    "lq2": [16, 64], "lk2": [16, 64], "gn_w": [16, 64], "gn_b": [16, 64],
    "w1": [8 * D, 4096], "b1": [8, 4096], "w2": [8 * 4096, D], "b2": [8, D],
    "wr": [D, 8], "br": [8],
}


def build_l0():
    kb = KB()
    xin = kb.dram("x_in", [T, D], F32)
    xout = kb.dram("x_out", [T, D], F32, kind="ExternalOutput")
    ato = kb.dram("ato_scr", [1024, T], BF16, kind="Internal")
    Wd = {nm: kb.dram("L0_" + nm, shp, F32) for nm, shp in L0_SHAPES.items()}
    cm = setup_common(kb)
    bs = make_bufs(kb)
    emit_load_x_fm(kb, bs, cm, xin, bs["X"])
    emit_l0(kb, bs, cm, Wd, ato)
    emit_store_x(kb, bs, cm, bs["X"], xout)
    nc = kb.build()
    return kb, nc


def l0_inmap(x_b, ins):
    m = {"x_in": np.ascontiguousarray(x_b)}
    for nm in L0_SHAPES:
        v = np.asarray(ins["L0_attn_" + nm]) if ("attn_" + nm) in \
            ["attn_" + k for k in ["Wq", "bq", "Wk", "bk", "Wv", "bv", "Wo",
                                   "bo", "lq1", "lk1", "lq2", "lk2", "gn_w",
                                   "gn_b"]] else None
    amap = {"Wq": "L0_attn_Wq", "bq": "L0_attn_bq", "Wk": "L0_attn_Wk",
            "bk": "L0_attn_bk", "Wv": "L0_attn_Wv", "bv": "L0_attn_bv",
            "Wo": "L0_attn_Wo", "bo": "L0_attn_bo", "lq1": "L0_attn_lq1",
            "lk1": "L0_attn_lk1", "lq2": "L0_attn_lq2", "lk2": "L0_attn_lk2",
            "gn_w": "L0_attn_gn_w", "gn_b": "L0_attn_gn_b",
            "w1": "L0_ffn_w1", "b1": "L0_ffn_b1", "w2": "L0_ffn_w2",
            "b2": "L0_ffn_b2", "wr": "L0_ffn_wr", "br": "L0_ffn_br"}
    for nm, key in amap.items():
        v = np.ascontiguousarray(np.asarray(ins[key], dtype=np.float32))
        v = v.reshape(L0_SHAPES[nm])
        m["L0_" + nm] = v
    return m



L1_SHAPES = {
    "td": [D], "tf": [D], "tmk": [D], "tmv": [D], "tmr": [D],
    "Wk": [D, D], "Wv": [D, D], "Wr": [D, D], "Wo": [D, D],
    "rw": [D, 16], "rb": [16],
    "egw": [15 * D, 512], "euw": [15 * D, 512], "edw": [15 * 512, D],
    "cw": [512 * D, 3], "suw": [D, 512], "sdw": [512, D],
}


def emit_wT_dj(kb, bs, cm, dram, dj, slot_raw, slot_out, n_cols=D):
    WS, PT = bs["WS"], bs["PT"]
    src = dram.base()[dj * 128:(dj + 1) * 128, :]
    kb.dma("pool", WS.f(slot_raw, slot_raw + n_cols),
           dram.wv(src), tag="wT_ld")
    nci = n_cols // 128
    for ci in range(nci):
        kb.transpose(PT.f(ci * 128, (ci + 1) * 128),
                     WS.f(slot_raw + ci * 128, slot_raw + (ci + 1) * 128),
                     cm["ident"], tag="wT_tr")
    kb.copy("dve", WS.f(slot_out, slot_out + n_cols), PT.f(0, n_cols),
            tag="wT_cp")


def emit_mix_proj(kb, bs, cm, Wdram, dj, acol, bcol, out_cb, extra_rank1=None):
    WS, XN, PP = bs["WS"], bs["XN"], bs["PP"]
    PM0, PM1 = bs["PM0"], bs["PM1"]
    emit_wT_dj(kb, bs, cm, Wdram, dj, 0, 1024)
    for ci in range(NCH):
        kb.ts("dve", WS.f(2048 + ci * 128, 2048 + (ci + 1) * 128),
              WS.f(1024 + ci * 128, 1024 + (ci + 1) * 128),
              PP.f(bcol + ci, bcol + ci + 1), None, ALU.mult, ALU.bypass,
              tag="mx_wb")
        kb.ts("dve", WS.f(1024 + ci * 128, 1024 + (ci + 1) * 128),
              WS.f(1024 + ci * 128, 1024 + (ci + 1) * 128),
              PP.f(acol + ci, acol + ci + 1), None, ALU.mult, ALU.bypass,
              tag="mx_wa")
    for tc in range(4):
        t0, t1 = tc * 512, (tc + 1) * 512
        pm = PM0 if tc % 2 == 0 else PM1
        for ci in range(NCH):
            kb.mm(pm.f(0, 512), WS.f(1024 + ci * 128, 1024 + (ci + 1) * 128),
                  XN.t(ci, t0, t1), start=(ci == 0), stop=False, tag="mx_a")
        for ci in range(NCH):
            last = (ci == NCH - 1) and extra_rank1 is None
            if tc == 0:
                kb.mm(pm.f(1, 512), WS.f(2048 + ci * 128,
                                         2048 + (ci + 1) * 128),
                      XN.t(ci, 0, 511), start=False, stop=last, tag="mx_b0")
            else:
                kb.mm(pm.f(0, 512), WS.f(2048 + ci * 128,
                                         2048 + (ci + 1) * 128),
                      XN.t(ci, t0 - 1, t1 - 1), start=False, stop=last,
                      tag="mx_b")
        out_cb(tc, pm)


def emit_l1(kb, bs, cm, Wd):
    X, XN, ACC, PBF = bs["X"], bs["XN"], bs["ACC"], bs["PBF"]
    WS, HB2, PP = bs["WS"], bs["HB2"], bs["PP"]
    PM0, PM1, PS0, PS1 = bs["PM0"], bs["PM1"], bs["PS0"], bs["PS1"]
    PO, PB, PR, PT = bs["PO"], bs["PB"], bs["PR"], bs["PT"]
    onesr, onesrf, onesc = cm["ones_r"], cm["ones_rf"], cm["ones_c"]

    emit_ln(kb, bs, cm, X, XN)
    for i, nm in enumerate(["td", "tf", "tmk", "tmv", "tmr"]):
        emit_vec_pp(kb, bs, cm, Wd[nm], i * 8)
    kb.act(pfTB(bs, 0, 8), PP.f(0, 8), AF.Exp, tag="l1_etd")
    kb.ts("dve", pfTB(bs, 0, 8), pfTB(bs, 0, 8), -1.0, None, ALU.mult,
          ALU.bypass, tag="l1_w")
    kb.act(PP.f(40, 48), pfTB(bs, 0, 8), AF.Exp, tag="l1_dec")
    kb.act(PP.f(48, 56), PP.f(8, 16), AF.Exp, tag="l1_eu")
    for i in range(3):
        kb.ts("dve", PP.f(56 + i * 8, 64 + i * 8), PP.f(16 + i * 8, 24 + i * 8),
              -1.0, 1.0, ALU.mult, ALU.add, tag="l1_om")

    for ch in range(NCH):
        kb.memset("pool", ACC.t(ch), 0.0, tag="acc_z")
    E1 = 16
    load_w_fm(kb, Wd["rw"], WS, n_in=D, n_out=E1, eng="pool", woff=2048)
    kb.dma("pool", WS.f(2176, 2192, 0, 1), Wd["rb"].wv(
        Wd["rb"].base().rearrange("(o d) -> o d", o=1)), tag="rb_ld")
    for tt in range(NTT):
        t0, t1 = tt * 128, (tt + 1) * 128
        for ci in range(NCH):
            kb.mm(PM0.f(0, E1), XN.t(ci, t0, t1),
                  WS.f(2048 + ci * E1, 2048 + (ci + 1) * E1),
                  start=(ci == 0), stop=False, tag="rt_mm")
        kb.mm(PM0.f(0, E1), onesr.f(), WS.f(2176, 2192, 0, 1),
              start=False, stop=True, tag="rt_bias")
        P_ = pfTB(bs, 0, 16)
        M1 = pfTB(bs, 32, 33)
        M2 = pfTB(bs, 33, 34)
        SU = pfTB(bs, 34, 35)
        NM = pfTB(bs, 35, 36)
        SM = pfTB(bs, 36, 37)
        kb.red("dve", NM, PM0.f(0, E1), ALU.max, negate=True, tag="rt_nmx")
        kb.act(P_, PM0.f(0, E1), AF.Exp, bias=NM, accum=SM, tag="rt_exp")
        kb.recip(SM, SM, tag="rt_rcs")
        kb.ts("dve", P_, P_, SM, None, ALU.mult, ALU.bypass, tag="rt_p")
        kb.red("dve", M1, P_, ALU.max, tag="rt_m1")
        MK1 = pfTB(bs, 48, 64)
        MK2 = pfTB(bs, 64, 80)
        E2 = pfTB(bs, 80, 96)
        kb.ts("dve", MK1, P_, M1, None, ALU.is_ge, ALU.bypass, tag="rt_mk1")
        kb.ts("dve", E2, MK1, -1.0, 1.0, ALU.mult, ALU.add, tag="rt_im")
        kb.tt("dve", E2, P_, E2, ALU.mult, tag="rt_e2")
        kb.red("dve", M2, E2, ALU.max, tag="rt_m2")
        kb.ts("dve", MK2, E2, M2, None, ALU.is_ge, ALU.bypass, tag="rt_mk2")
        kb.tt("dve", SU, M1, M2, ALU.add, tag="rt_su")
        kb.ts("dve", SU, SU, 1e-6, None, ALU.add, ALU.bypass, tag="rt_eps")
        kb.recip(SU, SU, tag="rt_rc")
        kb.tt("dve", M1, M1, SU, ALU.mult, tag="rt_w1")
        kb.tt("dve", M2, M2, SU, ALU.mult, tag="rt_w2")
        kb.ts("dve", MK1, MK1, M1, None, ALU.mult, ALU.bypass, tag="rt_g1")
        kb.ts("dve", MK2, MK2, M2, None, ALU.mult, ALU.bypass, tag="rt_g2")
        kb.tt("dve", MK1, MK1, MK2, ALU.add, tag="rt_gsum")
        kb.transpose(PR.f(0, 128, 0, E1), MK1, cm["identf32"], tag="rt_tr")
        kb.copy("dve", pfTD(bs, t0, t1, 0, E1), PR.f(0, 128, 0, E1),
                tag="rt_gev")

    cwsrc = Wd["cw"].base().rearrange("a k -> (a k)").rearrange(
        "(ii p c k) -> p ii c k", ii=4, p=128, c=D)
    cwdst = PBF.base()[:, 0:12288]
    kb.dma("pool", PBF.wv(cwdst, 0, 128, 0, 12288), Wd["cw"].wv(cwsrc),
           tag="cw_ld")
    for tc in range(4):
        t0, t1 = tc * 512, (tc + 1) * 512
        for ii in range(4):
            for tap in range(3):
                for ci in range(NCH):
                    src = PBF.base()[:, 0:12288].rearrange(
                        "p (ii c k) -> p ii c k", ii=4, c=D)[
                        :, ii, ci * 128:(ci + 1) * 128, tap]
                    kb.transpose(PT.f(ci * 128, (ci + 1) * 128),
                                 PBF.wv(src, 0, 128, ii * 3072,
                                        (ii + 1) * 3072),
                                 cm["ident"], tag="cv_tr")
                kb.copy("dve", PBF.f(12288 + tap * 1024, 12288 + tap * 1024
                                     + 1024), PT.f(0, 1024), tag="cv_cp")
            for tap in range(3):
                sh = 2 - tap
                for ci in range(NCH):
                    st = (tap == 0 and ci == 0)
                    sp = (tap == 2 and ci == NCH - 1)
                    lhs = PBF.f(12288 + tap * 1024 + ci * 128,
                                12288 + tap * 1024 + (ci + 1) * 128)
                    if t0 - sh >= 0:
                        kb.mm(PM0.f(0, 512), lhs, XN.t(ci, t0 - sh, t1 - sh),
                              start=st, stop=sp, tag="cv_mm")
                    else:
                        kb.mm(PM0.f(sh, 512), lhs, XN.t(ci, 0, t1 - sh),
                              start=st, stop=sp, tag="cv_mm0")
            kb.act(HB2.f(ii * 512, (ii + 1) * 512), PM0.f(0, 512), AF.Silu,
                   tag="cv_silu")
            load_w_slice(kb, Wd["suw"], WS, 0, ii, eng="pool")
            for ci in range(NCH):
                kb.mm(PM1.f(0, 512), WS.f(ci * 128, (ci + 1) * 128),
                      XN.t(ci, t0, t1), start=(ci == 0), stop=(ci == NCH - 1),
                      tag="up_mm")
            kb.tt("dve", HB2.f(ii * 512, (ii + 1) * 512),
                  HB2.f(ii * 512, (ii + 1) * 512), PM1.f(0, 512), ALU.mult,
                  tag="sh_h")
        for co in range(NCH):
            load_w_slice(kb, Wd["sdw"], WS, 0, co, n_in=512, eng="pool")
            for ii in range(4):
                kb.mm(PM0.f(0, 512), WS.f(ii * 128, (ii + 1) * 128),
                      HB2.f(ii * 512, (ii + 1) * 512), start=(ii == 0),
                      stop=(ii == 3), tag="sd_mm")
            kb.tt("dve", ACC.t(co, t0, t1), ACC.t(co, t0, t1), PM0.f(0, 512),
                  ALU.add, tag="sd_acc")

    for e in range(15):
        kb.dma("sp", pfTC(bs, 0, T, 0, 1), pfTD(bs, 0, T, e, e + 1),
               tag="ge_row")
        for tc in range(4):
            t0, t1 = tc * 512, (tc + 1) * 512
            kb.mm(PB.f(0, 512), onesrf.f(), pfTC(bs, t0, t1, 0, 1),
                  start=True, stop=True, tag="ge_bc")
            GBC = pfTB(bs, 0, 512)
            kb.copy("dve", GBC, PB.f(0, 512), tag="ge_bcc")
            for ii in range(4):
                slot = 1024 * (ii % 2)
                load_w_slice(kb, Wd["egw"], WS, slot, ii, n_in=D, eng="pool",
                             colw=128, row0=e * D, n_rows=D)
                pm = PM0 if ii % 2 == 0 else PM1
                for ci in range(NCH):
                    kb.mm(pm.f(0, 512), WS.f(slot + ci * 128,
                                             slot + (ci + 1) * 128),
                          XN.t(ci, t0, t1), start=(ci == 0),
                          stop=(ci == NCH - 1), tag="eg_mm")
                kb.act(HB2.f(ii * 512, (ii + 1) * 512), pm.f(0, 512), AF.Silu,
                       tag="eg_silu")
                load_w_slice(kb, Wd["euw"], WS, slot, ii, n_in=D, eng="pool",
                             colw=128, row0=e * D, n_rows=D)
                for ci in range(NCH):
                    kb.mm(pm.f(0, 512), WS.f(slot + ci * 128,
                                             slot + (ci + 1) * 128),
                          XN.t(ci, t0, t1), start=(ci == 0),
                          stop=(ci == NCH - 1), tag="eu_mm")
                kb.tt("dve", HB2.f(ii * 512, (ii + 1) * 512),
                      HB2.f(ii * 512, (ii + 1) * 512), pm.f(0, 512), ALU.mult,
                      tag="eu_h")
                kb.tt("dve", HB2.f(ii * 512, (ii + 1) * 512),
                      HB2.f(ii * 512, (ii + 1) * 512), GBC, ALU.mult,
                      tag="eu_g")
            for co in range(NCH):
                slot = 1024 * (co % 2)
                load_w_slice(kb, Wd["edw"], WS, slot, co, n_in=512, eng="pool",
                             colw=128, row0=e * 512, n_rows=512)
                pm = PM0 if co % 2 == 0 else PM1
                for ii in range(4):
                    kb.mm(pm.f(0, 512), WS.f(slot + ii * 128,
                                             slot + (ii + 1) * 128),
                          HB2.f(ii * 512, (ii + 1) * 512), start=(ii == 0),
                          stop=(ii == 3), tag="ed_mm")
                kb.tt("dve", ACC.t(co, t0, t1), ACC.t(co, t0, t1),
                      pm.f(0, 512), ALU.add, tag="ed_acc")

    kb.dma("sp", pfTC(bs, 0, T, 0, 1), pfTD(bs, 0, T, 15, 16), tag="ge_row15")
    for tc in range(4):
        t0, t1 = tc * 512, (tc + 1) * 512
        kb.mm(PB.f(0, 512), onesrf.f(), pfTC(bs, t0, t1, 0, 1),
              start=True, stop=True, tag="ge15_bc")
        GBC = pfTB(bs, 0, 512)
        kb.copy("dve", GBC, PB.f(0, 512), tag="ge15_bcc")
        for ch in range(NCH):
            kb.tt("dve", pfTA(bs, 0, 512), XN.t(ch, t0, t1), GBC, ALU.mult,
                  tag="id_mul")
            kb.tt("dve", ACC.t(ch, t0, t1), ACC.t(ch, t0, t1),
                  pfTA(bs, 0, 512), ALU.add, tag="id_acc")

    for dj in range(NCH):
        def evk(tc, pm, _dj=dj):
            kb.copy("dve", pfTA(bs, tc * 512, (tc + 1) * 512), pm.f(0, 512),
                    tag="k_ev")
        emit_mix_proj(kb, bs, cm, Wd["Wk"], dj, 16, 56, evk)
        def evv(tc, pm, _dj=dj):
            kb.copy("dve", pfTB(bs, tc * 512, (tc + 1) * 512), pm.f(0, 512),
                    tag="v_ev")
        emit_mix_proj(kb, bs, cm, Wd["Wv"], dj, 24, 64, evv)
        def evr(tc, pm, _dj=dj):
            kb.act(HB2.f(tc * 512, (tc + 1) * 512), pm.f(0, 512), AF.Sigmoid,
                   tag="r_ev")
        emit_mix_proj(kb, bs, cm, Wd["Wr"], dj, 32, 72, evr)
        M_ = pfTC(bs, 0, 1)
        kb.red("dve", M_, pfTA(bs, 0, T), ALU.max, tag="wk_m")
        kb.ts("dve", pfTC(bs, 1, 2), PP.f(8 + dj, 9 + dj), 0.0, None,
              ALU.max, ALU.bypass, tag="wk_ur")
        kb.tt("dve", M_, M_, pfTC(bs, 1, 2), ALU.add, tag="wk_mu")
        kb.ts("dve", pfTC(bs, 1, 2), M_, -1.0, None, ALU.mult, ALU.bypass,
              tag="wk_nm")
        kb.act(pfTA(bs, 0, T), pfTA(bs, 0, T), AF.Exp,
               bias=pfTC(bs, 1, 2), tag="wk_kk")
        kb.tt("dve", pfTC(bs, 0, T), pfTA(bs, 0, T), pfTB(bs, 0, T),
              ALU.mult, tag="wk_aN")
        DECB = PP.f(40 + dj, 41 + dj).ap.broadcast_to([128, T])
        DECV = View(bs["PP"], DECB, 0, 128, 40 + dj, 41 + dj)
        kb.scan("dve", pfTD(bs, 0, T), DECV, pfTC(bs, 0, T), 0.0,
                ALU.mult, ALU.add, tag="wk_hN")
        kb.scan("dve", pfTC(bs, 0, T), DECV, pfTA(bs, 0, T), 0.0,
                ALU.mult, ALU.add, tag="wk_hD")
        kb.ts("dve", pfTA(bs, 0, T), pfTA(bs, 0, T), PP.f(48 + dj, 49 + dj),
              None, ALU.mult, ALU.bypass, tag="wk_bon")
        kb.tt("dve", pfTB(bs, 0, T), pfTA(bs, 0, T), pfTB(bs, 0, T),
              ALU.mult, tag="wk_bv")
        kb.tt("dve", pfTB(bs, 1, T), pfTB(bs, 1, T), pfTD(bs, 0, T - 1),
              ALU.add, tag="wk_nm2")
        kb.tt("dve", pfTA(bs, 1, T), pfTA(bs, 1, T), pfTC(bs, 0, T - 1),
              ALU.add, tag="wk_dn")
        kb.ts("dve", pfTA(bs, 0, T), pfTA(bs, 0, T), 1e-38, None, ALU.add,
              ALU.bypass, tag="wk_eps")
        kb.recip(pfTA(bs, 0, T), pfTA(bs, 0, T), tag="wk_rc")
        kb.tt("dve", pfTD(bs, 0, T), pfTB(bs, 0, T), pfTA(bs, 0, T),
              ALU.mult, tag="wk_y")
        kb.tt("dve", PBF.f(dj * T, (dj + 1) * T), pfTD(bs, 0, T),
              HB2.f(0, T), ALU.mult, tag="wk_sr")

    for dj in range(NCH):
        emit_wT_dj(kb, bs, cm, Wd["Wo"], dj, 0, 1024)
        for tc in range(4):
            t0, t1 = tc * 512, (tc + 1) * 512
            pm = PM0 if tc % 2 == 0 else PM1
            for ci in range(NCH):
                kb.mm(pm.f(0, 512), WS.f(1024 + ci * 128,
                                         1024 + (ci + 1) * 128),
                      PBF.f(ci * T + t0, ci * T + t1), start=(ci == 0),
                      stop=(ci == NCH - 1), tag="so_mm")
            kb.act(pfTC(bs, 0, 512), ACC.t(dj, t0, t1), AF.Silu,
                   tag="res_silu")
            kb.tt("dve", pfTB(bs, 0, 512), pm.f(0, 512), pfTC(bs, 0, 512),
                  ALU.mult, tag="res_mul")
            kb.tt("dve", X.t(dj, t0, t1), X.t(dj, t0, t1), pfTB(bs, 0, 512),
                  ALU.add, tag="res_add")


def build_l1():
    kb = KB()
    xin = kb.dram("x_in", [T, D], F32)
    xout = kb.dram("x_out", [T, D], F32, kind="ExternalOutput")
    Wd = {nm: kb.dram("L1_" + nm, shp, F32) for nm, shp in L1_SHAPES.items()}
    cm = setup_common(kb)
    bs = make_bufs(kb)
    emit_load_x_fm(kb, bs, cm, xin, bs["X"])
    emit_l1(kb, bs, cm, Wd)
    emit_store_x(kb, bs, cm, bs["X"], xout)
    nc = kb.build()
    return kb, nc


def l1_inmap(x_b, ins):
    m = {"x_in": np.ascontiguousarray(x_b)}
    amap = {"td": "L1_attn_time_decay", "tf": "L1_attn_time_first",
            "tmk": "L1_attn_time_mix_k", "tmv": "L1_attn_time_mix_v",
            "tmr": "L1_attn_time_mix_r", "Wk": "L1_attn_W_key",
            "Wv": "L1_attn_W_value", "Wr": "L1_attn_W_receptance",
            "Wo": "L1_attn_W_output", "rw": "L1_ffn_router_w",
            "rb": "L1_ffn_router_bias", "egw": "L1_ffn_expert_gate_w",
            "euw": "L1_ffn_expert_up_w", "edw": "L1_ffn_expert_down_w",
            "cw": "L1_ffn_conv_w", "suw": "L1_ffn_shared_up_w",
            "sdw": "L1_ffn_shared_down_w"}
    for nm, key in amap.items():
        v = np.ascontiguousarray(np.asarray(ins[key], dtype=np.float32))
        m["L1_" + nm] = v.reshape(L1_SHAPES[nm])
    return m



L3_SHAPES = {
    "Wq": [D, D], "Wk": [D, D], "Wv": [D, D], "Wr": [D, D],
    "ub": [16, 64], "vb": [16, 64],
    "fWr": [D, D], "fWk": [D, D], "fWv": [D, D], "mur": [D], "muk": [D],
    "rtab": [D, 4096],
}


def emit_l3(kb, bs, cm, Wd, bd_scr):
    X, XN, ACC, PBF = bs["X"], bs["XN"], bs["ACC"], bs["PBF"]
    WS, HB2, PP = bs["WS"], bs["HB2"], bs["PP"]
    PM0, PM1, PS0, PS1 = bs["PM0"], bs["PM1"], bs["PS0"], bs["PS1"]
    PO, PB, PR, PT = bs["PO"], bs["PB"], bs["PR"], bs["PT"]
    onesr, onesrf, onesc = cm["ones_r"], cm["ones_rf"], cm["ones_c"]
    M = T - 1

    emit_ln(kb, bs, cm, X, XN)
    emit_vec_pp(kb, bs, cm, Wd["mur"], 0)
    emit_vec_pp(kb, bs, cm, Wd["muk"], 8)
    for i in range(2):
        kb.ts("dve", PP.f(16 + i * 8, 24 + i * 8), PP.f(i * 8, 8 + i * 8),
              -1.0, 1.0, ALU.mult, ALU.add, tag="l3_om")
    emit_vec_pp(kb, bs, cm, Wd["ub"], 32)
    emit_vec_pp(kb, bs, cm, Wd["vb"], 40)
    kb.memset("dve", PP.f(48, 49), 0.125, tag="l3_sc")

    for dj in range(NCH):
        def evr(tc, pm, _dj=dj):
            kb.act(ACC.t(_dj, tc * 512, (tc + 1) * 512), pm.f(0, 512),
                   AF.Sigmoid, tag="f_r")
        emit_mix_proj(kb, bs, cm, Wd["fWr"], dj, 0, 16, evr)
    for dj in range(NCH):
        def evk(tc, pm, _dj=dj):
            kb.ts("dve", pfTA(bs, 0, 512), pm.f(0, 512), 0.0, None,
                  ALU.max, ALU.bypass, tag="f_kr")
            kb.act(PBF.f(_dj * T + tc * 512, _dj * T + (tc + 1) * 512),
                   pfTA(bs, 0, 512), AF.Square, tag="f_k2")
        emit_mix_proj(kb, bs, cm, Wd["fWk"], dj, 8, 24, evk)
    for dj in range(NCH):
        emit_wT_dj(kb, bs, cm, Wd["fWv"], dj, 0, 1024)
        for tc in range(4):
            t0, t1 = tc * 512, (tc + 1) * 512
            pm = PM0 if tc % 2 == 0 else PM1
            for ci in range(NCH):
                kb.mm(pm.f(0, 512), WS.f(1024 + ci * 128,
                                         1024 + (ci + 1) * 128),
                      PBF.f(ci * T + t0, ci * T + t1), start=(ci == 0),
                      stop=(ci == NCH - 1), tag="fv_mm")
            kb.tt("dve", pfTA(bs, 0, 512), pm.f(0, 512), ACC.t(dj, t0, t1),
                  ALU.mult, tag="f_rv")
            kb.act(ACC.t(dj, t0, t1), pfTA(bs, 0, 512), AF.Silu, tag="f_g")

    QO, KO, VO, RPO, QVO, VTO, BDO = 0, 2048, 4096, 6144, 10240, 12288, 13312
    for dj in range(NCH):
        for w_i, (wnm, dsto) in enumerate([("Wq", QO), ("Wk", KO),
                                           ("Wv", VO)]):
            emit_wT_dj(kb, bs, cm, Wd[wnm], dj, 0, 1024)
            for tc in range(4):
                t0, t1 = tc * 512, (tc + 1) * 512
                pm = PM0 if tc % 2 == 0 else PM1
                for ci in range(NCH):
                    kb.mm(pm.f(0, 512), WS.f(1024 + ci * 128,
                                             1024 + (ci + 1) * 128),
                          XN.t(ci, t0, t1), start=(ci == 0),
                          stop=(ci == NCH - 1), tag="qkv3_mm")
                kb.copy("dve", PBF.f(dsto + t0, dsto + t1), pm.f(0, 512),
                        tag="qkv3_ev")
        emit_wT_dj(kb, bs, cm, Wd["Wr"], dj, 0, 1024)
        for rc in range(8):
            r0, r1 = rc * 512, (rc + 1) * 512
            pm = PM0 if rc % 2 == 0 else PM1
            for ci in range(NCH):
                src = Wd["rtab"].base()[ci * 128:(ci + 1) * 128, r0:r1]
                kb.dma("pool", WS.f(2048 + (ci % 2) * 512,
                                    2048 + (ci % 2) * 512 + 512),
                       Wd["rtab"].wv(src), tag="rt_ld")
                kb.mm(pm.f(0, 512), WS.f(1024 + ci * 128,
                                         1024 + (ci + 1) * 128),
                      WS.f(2048 + (ci % 2) * 512, 2048 + (ci % 2) * 512 + 512),
                      start=(ci == 0), stop=(ci == NCH - 1), tag="rp_mm")
            kb.copy("dve", PBF.f(RPO + r0, RPO + r1), pm.f(0, 512),
                    tag="rp_ev")
        for half in range(2):
            h = 2 * dj + half
            p0, p1 = half * 64, half * 64 + 64
            tp = None if half == 0 else (0, 64)
            idv = cm["ident"] if half == 0 else cm["ident64"]
            for rnd in range(2):
                for j8 in range(8):
                    jt = rnd * 8 + j8
                    kb.transpose(PT.f(j8 * 128, (j8 + 1) * 128)
                                 if False else PT.f(j8 * 64, (j8 + 1) * 64),
                                 PBF.f(VO + jt * 128, VO + (jt + 1) * 128,
                                       p0, p1), idv, tag="vt_tr")
                kb.copy("dve", PBF.f(VTO + rnd * 512, VTO + (rnd + 1) * 512),
                        PT.f(0, 512), tag="vt_ev")
            kb.ts("dve", PBF.f(QVO, QVO + T, p0, p1),
                  PBF.f(QO, QO + T, p0, p1), PP.f(40 + dj, 41 + dj, p0, p1),
                  None, ALU.add, ALU.bypass, tag="qv_add")
            for it in range(NTT):
                i0 = it * 128
                w0 = M - i0 - 127
                widths = [512, 512, 512, 512, 127]
                off = 0
                for c5, wd in enumerate(widths):
                    pm = PS0 if c5 % 2 == 0 else PS1
                    stg = 2048 + (c5 % 2) * 512
                    kb.mm(pm.f(0, wd), PBF.f(QVO + i0, QVO + i0 + 128, p0, p1),
                          PBF.f(RPO + w0 + off, RPO + w0 + off + wd, p0, p1),
                          start=True, stop=True, tag="pre_mm")
                    kb.copy("dve", WS.f(stg, stg + wd), pm.f(0, wd),
                            tag="pre_ev")
                    dst = bd_scr.base()[it * 128:(it + 1) * 128, off:off + wd]
                    kb.dma("sp", bd_scr.wv(dst), WS.f(stg, stg + wd),
                           tag="pre_st")
                    off += wd
            kb.ts("dve", PBF.f(QO, QO + T, p0, p1), PBF.f(QO, QO + T, p0, p1),
                  PP.f(32 + dj, 33 + dj, p0, p1), None, ALU.add, ALU.bypass,
                  tag="qu_add")
            for it in range(NTT):
                i0 = it * 128
                W_ = 2560
                base_el = it * 128 * W_ + 127
                ap = bass.AP(bd_scr.th, base_el, [[W_ - 1, 128], [1, T]])
                kb.dma("sp", PBF.f(BDO, BDO + T), bd_scr.wv(ap), tag="bd_ld")
                for jc in range(4):
                    pm = PM0 if jc % 2 == 0 else PM1
                    kb.mm(pm.f(0, 512), PBF.f(QO + i0, QO + i0 + 128, p0, p1),
                          PBF.f(KO + jc * 512, KO + (jc + 1) * 512, p0, p1),
                          start=True, stop=True, tag="ac_mm")
                    kb.tt("dve", pfTA(bs, jc * 512, (jc + 1) * 512),
                          pm.f(0, 512), PBF.f(BDO + jc * 512,
                                              BDO + (jc + 1) * 512),
                          ALU.add, tag="sc_add")
                kb.red("dve", pfTB(bs, 700, 701), pfTA(bs, 0, T), ALU.max,
                       negate=True, tag="sm_nmx")
                kb.ts("dve", pfTB(bs, 700, 701), pfTB(bs, 700, 701), 0.125,
                      None, ALU.mult, ALU.bypass, tag="sm_sb")
                kb.act(PBF.f(BDO, BDO + T), pfTA(bs, 0, T), AF.Exp,
                       bias=pfTB(bs, 700, 701), scale=PP.f(48, 49),
                       accum=pfTB(bs, 701, 702), tag="sm_exp")
                kb.recip(pfTB(bs, 701, 702), pfTB(bs, 701, 702), tag="sm_rc")
                kb.ts("dve", PBF.f(BDO, BDO + T), PBF.f(BDO, BDO + T),
                      pfTB(bs, 701, 702), None, ALU.mult, ALU.bypass,
                      tag="sm_nrm")
                for rnd in range(2):
                    for j8 in range(8):
                        jt = rnd * 8 + j8
                        kb.transpose(PT.f(j8 * 128, (j8 + 1) * 128),
                                     PBF.f(BDO + jt * 128,
                                           BDO + (jt + 1) * 128),
                                     cm["ident"], tag="av_tr")
                    kb.copy("dve", WS.f(1024, 2048), PT.f(0, 1024),
                            tag="av_cp")
                    for j8 in range(8):
                        jt = rnd * 8 + j8
                        kb.mm(PO.f(0, 128, p0, p1) if half else
                              PO.f(0, 128, 0, 64),
                              PBF.f(VTO + jt * 64, VTO + (jt + 1) * 64),
                              WS.f(1024 + j8 * 128, 1024 + (j8 + 1) * 128),
                              start=(jt == 0), stop=(jt == NTT - 1),
                              tile_position=tp, tag="av3_mm")
                kb.act(pfTC(bs, 0, 128, p0, p1), ACC.t(dj, i0, i0 + 128,
                                                       p0, p1), AF.Silu,
                       tag="res3_silu")
                kb.tt("dve", pfTB(bs, 0, 128, p0, p1),
                      PO.f(0, 128, p0, p1) if half else PO.f(0, 128, 0, 64),
                      pfTC(bs, 0, 128, p0, p1), ALU.mult, tag="res3_mul")
                kb.tt("dve", X.t(dj, i0, i0 + 128, p0, p1),
                      X.t(dj, i0, i0 + 128, p0, p1),
                      pfTB(bs, 0, 128, p0, p1), ALU.add, tag="res3_add")


def build_l3():
    kb = KB()
    xin = kb.dram("x_in", [T, D], F32)
    xout = kb.dram("x_out", [T, D], F32, kind="ExternalOutput")
    bd_scr = kb.dram("bd_scr", [T, 2560], BF16, kind="Internal")
    Wd = {}
    for nm, shp in L3_SHAPES.items():
        Wd[nm] = kb.dram("L3_" + nm, shp, BF16 if nm == "rtab" else F32)
    cm = setup_common(kb)
    bs = make_bufs(kb)
    emit_load_x_fm(kb, bs, cm, xin, bs["X"])
    emit_l3(kb, bs, cm, Wd, bd_scr)
    emit_store_x(kb, bs, cm, bs["X"], xout)
    nc = kb.build()
    return kb, nc


def make_rtab():
    inv_freq = 1.0 / 10000.0 ** (np.arange(0, D, 2, dtype=np.float32) / D)
    pos = np.arange(-(T - 1), T, dtype=np.float32)
    ang = pos[:, None] * inv_freq[None, :]
    emb = np.zeros((2 * T - 1, D), dtype=np.float32)
    emb[:, 0::2] = np.sin(ang)
    emb[:, 1::2] = np.cos(ang)
    out = np.zeros((D, 4096), dtype=np.float32)
    out[:, 0:2 * T - 1] = emb.T
    return out.astype(BF)


def l3_inmap(x_b, ins, rtab):
    m = {"x_in": np.ascontiguousarray(x_b), "L3_rtab": rtab}
    amap = {"Wq": "L3_attn_Wq", "Wk": "L3_attn_Wk", "Wv": "L3_attn_Wv",
            "Wr": "L3_attn_Wr", "ub": "L3_attn_u_bias", "vb": "L3_attn_v_bias",
            "fWr": "L3_ffn_W_r", "fWk": "L3_ffn_W_k", "fWv": "L3_ffn_W_v",
            "mur": "L3_ffn_mu_r", "muk": "L3_ffn_mu_k"}
    for nm, key in amap.items():
        v = np.ascontiguousarray(np.asarray(ins[key], dtype=np.float32))
        m["L3_" + nm] = v.reshape(L3_SHAPES[nm])
    return m



L2_SHAPES = {
    "cvw": [D, 4], "cvb": [D], "Wrw": [D, D], "Wrb": [D], "Wiw": [D, D],
    "Wib": [D], "loga": [D], "outw": [D, D], "nw": [D],
    "gw": [32, D], "wgu": [32 * D, D], "wdn": [32 * 512, D],
    "sgu": [D, D], "sdn": [512, D],
}


def emit_l2a(kb, bs, cm, Wd, ow_out, cnt_out):
    X, XN, PP, WS = bs["X"], bs["XN"], bs["PP"], bs["WS"]
    PM0, PR = bs["PM0"], bs["PR"]
    emit_ln(kb, bs, cm, X, XN)
    src = Wd["gw"].base()
    kb.dma("pool", WS.f(2048, 3072, 0, 32), Wd["gw"].wv(src), tag="gw_ld")
    for ci in range(NCH):
        kb.transpose(bs["PT"].f(ci * 32, (ci + 1) * 32),
                     WS.f(2048 + ci * 128, 2048 + (ci + 1) * 128, 0, 32),
                     cm["ident"], tag="gw_tr")
    kb.copy("dve", WS.f(0, 256), bs["PT"].f(0, 256), tag="gw_cp")
    for tt in range(NTT):
        t0, t1 = tt * 128, (tt + 1) * 128
        for ci in range(NCH):
            kb.mm(PM0.f(0, 32), XN.t(ci, t0, t1), WS.f(ci * 32, (ci + 1) * 32),
                  start=(ci == 0), stop=(ci == NCH - 1), tag="l2r_mm")
        P_ = pfTB(bs, 0, 32)
        NMX = pfTB(bs, 40, 41)
        SM = pfTB(bs, 41, 42)
        kb.red("dve", NMX, PM0.f(0, 32), ALU.max, negate=True, tag="l2r_nmx")
        kb.act(P_, PM0.f(0, 32), AF.Exp, bias=NMX, accum=SM, tag="l2r_exp")
        kb.recip(SM, SM, tag="l2r_rc")
        kb.ts("dve", P_, P_, SM, None, ALU.mult, ALU.bypass, tag="l2r_p")
        E_ = pfTB(bs, 64, 96)
        OW = pfTB(bs, 96, 128)
        kb.copy("dve", E_, P_, tag="l2r_e0")
        kb.memset("dve", OW, 0.0, tag="l2r_ow0")
        SU4 = pfTB(bs, 42, 43)
        kb.memset("dve", SU4, 0.0, tag="l2r_s40")
        MKS = []
        for rnd in range(4):
            Mr = pfTB(bs, 43 + rnd, 44 + rnd)
            MKr = pfTB(bs, 128 + rnd * 32, 160 + rnd * 32)
            kb.red("dve", Mr, E_, ALU.max, tag="l2r_m")
            kb.ts("dve", MKr, E_, Mr, None, ALU.is_ge, ALU.bypass,
                  tag="l2r_mk")
            kb.tt("dve", SU4, SU4, Mr, ALU.add, tag="l2r_s4")
            IM = pfTB(bs, 48, 80) if False else pfTB(bs, 256, 288)
            kb.ts("dve", IM, MKr, -1.0, 1.0, ALU.mult, ALU.add, tag="l2r_im")
            kb.tt("dve", E_, E_, IM, ALU.mult, tag="l2r_e")
            MKS.append((Mr, MKr))
        kb.recip(SU4, SU4, tag="l2r_rs4")
        for rnd in range(4):
            Mr, MKr = MKS[rnd]
            kb.tt("dve", Mr, Mr, SU4, ALU.mult, tag="l2r_w")
            kb.ts("dve", MKr, MKr, Mr, None, ALU.mult, ALU.bypass,
                  tag="l2r_gw")
            kb.tt("dve", OW, OW, MKr, ALU.add, tag="l2r_acc")
        dst = ow_out.base()[t0:t1, :]
        kb.dma("sp", ow_out.wv(dst, t0 * 32, t1 * 32), OW, tag="ow_st")
        O01 = pfTB(bs, 288, 320)
        kb.tss("dve", O01, OW, 0.0, ALU.is_gt, tag="l2r_o01")
        kb.mm(PR.f(0, 32, 0, 1), cm["ones_cf"].f(), O01,
              start=(tt == 0), stop=(tt == NTT - 1), tag="l2r_cnt")
    kb.copy("dve", pfTB(bs, 320, 352, 0, 1), PR.f(0, 32, 0, 1), tag="cnt_ev")
    kb.dma("sp", cnt_out.whole(), pfTB(bs, 320, 352, 0, 1), tag="cnt_st")


L2B_PHASE = 99


def emit_l2b(kb, bs, cm, Wd, owt, cnt8, pmask):
    X, XN, ACC, PBF = bs["X"], bs["XN"], bs["ACC"], bs["PBF"]
    WS, HB2, PP = bs["WS"], bs["HB2"], bs["PP"]
    PM0, PM1, PS0, PS1 = bs["PM0"], bs["PM1"], bs["PS0"], bs["PS1"]
    PO, PB, PR, PT = bs["PO"], bs["PB"], bs["PR"], bs["PT"]
    onesr, onesrf, onesc = cm["ones_r"], cm["ones_rf"], cm["ones_c"]
    LN1E7 = float(np.log(np.float32(1e-7)))

    emit_ln(kb, bs, cm, X, XN)
    cvsrc = Wd["cvw"].base().rearrange("a k -> (a k)").rearrange(
        "(ch p k) -> p ch k", p=128, k=4)
    stage = pfTD(bs, 0, 32, 0, 128)
    kb.dma("sp", stage.buf.wv(
        bs["PF32"].base()[:, 3 * T:3 * T + 32].rearrange(
            "p (ch k) -> p ch k", ch=8), 0, 128, 3 * T, 3 * T + 32),
        Wd["cvw"].wv(cvsrc), tag="cv_ld")
    for tap in range(4):
        for ch in range(NCH):
            kb.copy("dve", PP.f(tap * 8 + ch, tap * 8 + ch + 1),
                    pfTD(bs, ch * 4 + tap, ch * 4 + tap + 1), tag="cv_pp")
    emit_vec_pp(kb, bs, cm, Wd["cvb"], 32)
    emit_vec_pp(kb, bs, cm, Wd["loga"], 40)
    kb.act(PP.f(40, 48), PP.f(40, 48), AF.Sigmoid, tag="l2_ab")
    kb.act(PP.f(40, 48), PP.f(40, 48), AF.Ln, tag="l2_lnab")
    kb.ts("dve", PP.f(40, 48), PP.f(40, 48), 8.0, None, ALU.mult, ALU.bypass,
          tag="l2_8ln")
    emit_vec_pp(kb, bs, cm, Wd["nw"], 48)
    emit_vec_pp(kb, bs, cm, Wd["Wrb"], 56)
    emit_vec_pp(kb, bs, cm, Wd["Wib"], 64)
    kb.memset("dve", PP.f(72, 73), -80.0, tag="l2_m80")
    kb.memset("dve", PP.f(73, 74), 0.0, tag="l2_z")

    if L2B_PHASE < 1:
        return
    for ch in range(NCH):
        bc = PP.f(32 + ch, 33 + ch)
        kb.ts("dve", pfTA(bs, 0, T), XN.t(ch), PP.f(24 + ch, 25 + ch), None,
              ALU.mult, ALU.bypass, tag="xc_k3")
        for tap in range(3):
            sh = 3 - tap
            kb.stt("dve", pfTA(bs, sh, T), XN.t(ch, 0, T - sh),
                   PP.f(tap * 8 + ch, tap * 8 + ch + 1), pfTA(bs, sh, T),
                   ALU.mult, ALU.add, tag="xc_tap")
        kb.ts("dve", ACC.t(ch), pfTA(bs, 0, T), bc, None, ALU.add, ALU.bypass,
              tag="xc_b")
    for dj in range(NCH):
        for wnm, bcol, dst in [("Wrw", 56, "TA"), ("Wiw", 64, "TB")]:
            emit_wT_dj(kb, bs, cm, Wd[wnm], dj, 0, 1024)
            for tc in range(4):
                t0, t1 = tc * 512, (tc + 1) * 512
                pm = PM0 if tc % 2 == 0 else PM1
                for ci in range(NCH):
                    kb.mm(pm.f(0, 512), WS.f(1024 + ci * 128,
                                             1024 + (ci + 1) * 128),
                          ACC.t(ci, t0, t1), start=(ci == 0),
                          stop=(ci == NCH - 1), tag="ri_mm")
                dv = pfTA(bs, t0, t1) if dst == "TA" else pfTB(bs, t0, t1)
                kb.act(dv, pm.f(0, 512), AF.Sigmoid,
                       bias=PP.f(bcol + dj, bcol + dj + 1), tag="ri_sg")
        kb.ts("dve", pfTA(bs, 0, T), pfTA(bs, 0, T), PP.f(40 + dj, 41 + dj),
              None, ALU.mult, ALU.bypass, tag="sc_z")
        kb.act(pfTC(bs, 0, T), pfTA(bs, 0, T), AF.Exp, tag="sc_a")
        kb.tt("dve", pfTD(bs, 0, T), pfTC(bs, 0, T), pfTC(bs, 0, T),
              ALU.mult, tag="sc_a2")
        kb.ts("dve", pfTD(bs, 0, T), pfTD(bs, 0, T), -1.0, 1.0, ALU.mult,
              ALU.add, tag="sc_1a")
        kb.ts("dve", pfTD(bs, 0, T), pfTD(bs, 0, T), 1e-6, None, ALU.max,
              ALU.bypass, tag="sc_cl")
        kb.act(pfTD(bs, 0, T), pfTD(bs, 0, T), AF.Sqrt, tag="sc_sq")
        kb.tt("dve", pfTB(bs, 0, T), pfTB(bs, 0, T), ACC.t(dj), ALU.mult,
              tag="sc_ix")
        kb.tt("dve", pfTD(bs, 0, T), pfTD(bs, 0, T), pfTB(bs, 0, T),
              ALU.mult, tag="sc_b")
        kb.ts("dve", pfTA(bs, 0, T), pfTA(bs, 0, T), LN1E7, None, ALU.max,
              ALU.bypass, tag="sc_zc")
        M80 = View(bs["PP"], PP.f(72, 73).ap.broadcast_to([128, T]), 0, 128,
                   72, 73)
        kb.scan("dve", pfTB(bs, 0, T), pfTA(bs, 0, T), M80, 0.0, ALU.add,
                ALU.max, tag="sc_lA")
        kb.act(pfTA(bs, 0, T), pfTB(bs, 0, T), AF.Exp, tag="sc_A")
        kb.ts("dve", pfTB(bs, 0, T), pfTB(bs, 0, T), -1.0, None, ALU.mult,
              ALU.bypass, tag="sc_nlA")
        kb.act(pfTB(bs, 0, T), pfTB(bs, 0, T), AF.Exp, tag="sc_eA")
        kb.tt("dve", pfTD(bs, 0, T), pfTD(bs, 0, T), pfTB(bs, 0, T),
              ALU.mult, tag="sc_u")
        Z0 = View(bs["PP"], PP.f(73, 74).ap.broadcast_to([128, T]), 0, 128,
                  73, 74)
        kb.scan("dve", pfTD(bs, 0, T), pfTD(bs, 0, T), Z0, 0.0, ALU.add,
                ALU.add, tag="sc_cs")
        kb.tt("dve", pfTA(bs, 0, T), pfTA(bs, 0, T), pfTD(bs, 0, T),
              ALU.mult, tag="sc_h")
        kb.copy("dve", PBF.f(dj * T, (dj + 1) * T), pfTA(bs, 0, T),
                tag="sc_hb")
    if L2B_PHASE < 2:
        return
    kb.memset("dve", pfTA(bs, 0, T, 0, 1), 0.0, tag="ms_z")
    for dj in range(NCH if L2B_PHASE >= 3 else 1):
        emit_wT_dj(kb, bs, cm, Wd["outw"], dj, 0, 1024)
        for tc in range(4):
            t0, t1 = tc * 512, (tc + 1) * 512
            pm = PM0 if tc % 2 == 0 else PM1
            for ci in range(NCH):
                kb.mm(pm.f(0, 512), WS.f(1024 + ci * 128,
                                         1024 + (ci + 1) * 128),
                      PBF.f(ci * T + t0, ci * T + t1), start=(ci == 0),
                      stop=(ci == NCH - 1), tag="y2_mm")
            kb.copy("dve", ACC.t(dj, t0, t1), pm.f(0, 512), tag="y2_ev")
            kb.act(HB2.f(tc * 512, (tc + 1) * 512), pm.f(0, 512), AF.Square,
                   tag="y2_sq")
            if L2B_PHASE != 2:
                kb.mm(PR.f(0, 512, 0, 1), onesc.f(),
                      HB2.f(tc * 512, (tc + 1) * 512),
                      start=True, stop=True, tag="y2_ss")
                kb.tt("dve", pfTA(bs, t0, t1, 0, 1), pfTA(bs, t0, t1, 0, 1),
                      PR.f(0, 512, 0, 1), ALU.add, tag="y2_msacc")
    kb.ts("dve", pfTA(bs, 0, T, 0, 1), pfTA(bs, 0, T, 0, 1), 1.0 / D, None,
          ALU.mult, ALU.bypass, tag="y2_ms")
    if L2B_PHASE == 2:
        return
    kb.ts("dve", pfTA(bs, 0, T, 0, 1), pfTA(bs, 0, T, 0, 1), 1e-6, None,
          ALU.add, ALU.bypass, tag="rms_eps")
    kb.act(pfTA(bs, 0, T, 0, 1), pfTA(bs, 0, T, 0, 1), AF.Sqrt, tag="rms_sd")
    kb.recip(pfTA(bs, 0, T, 0, 1), pfTA(bs, 0, T, 0, 1), tag="rms_rc")
    for tc in range(4):
        t0, t1 = tc * 512, (tc + 1) * 512
        kb.mm(PB.f(0, 512), onesrf.f(), pfTA(bs, t0, t1, 0, 1), start=True,
              stop=True, tag="rms_bc")
        for dj in range(NCH):
            kb.tt("dve", ACC.t(dj, t0, t1), ACC.t(dj, t0, t1), PB.f(0, 512),
                  ALU.mult, tag="rms_ap")
            kb.ts("dve", ACC.t(dj, t0, t1), ACC.t(dj, t0, t1),
                  PP.f(48 + dj, 49 + dj), None, ALU.mult, ALU.bypass,
                  tag="rms_nw")

    if L2B_PHASE < 3:
        return
    kb.dma("sp", pfTA(bs, 1900, 1932, 0, 8), cnt8.whole(), tag="c8_ld")
    kb.dma("sp", pfTA(bs, 1932, 1933, 0, 8), pmask.whole(), tag="pm_ld")
    kb.mm(PR.f(0, 32, 0, 1), pfTA(bs, 1932, 1933, 0, 8),
          pfTA(bs, 1900, 1932, 0, 8), start=True, stop=True, tag="base_mm")
    kb.copy("dve", pfTA(bs, 1933, 1965, 0, 1), PR.f(0, 32, 0, 1),
            tag="base_ev")
    kb.transpose(PR.f(0, 1, 0, 32), pfTA(bs, 1933, 1965, 0, 1),
                 cm["identf32"], tag="base_tr")
    kb.copy("dve", pfTA(bs, 1965, 1966, 0, 32), PR.f(0, 1, 0, 32),
            tag="base_cp")
    kb.dma("sp", pfTD(bs, 0, T, 0, 32), owt.whole(), tag="owt_ld")
    O01 = pfTC(bs, 0, T, 0, 32)
    kb.tss("dve", O01, pfTD(bs, 0, T, 0, 32), 0.0, ALU.is_gt, tag="o01")
    PX = pfTB(bs, 0, T, 0, 32)
    Z0b = View(bs["PP"], PP.f(73, 74, 0, 32).ap.broadcast_to([32, T]), 0, 32,
               73, 74)
    kb.scan("dve", PX, O01, Z0b, 0.0, ALU.add, ALU.add, tag="p_scan")
    kb.ts("dve", PX, PX, pfTA(bs, 1965, 1966, 0, 32), None, ALU.add,
          ALU.bypass, tag="p_base")
    kb.tss("dve", O01, PX, 1024.0, ALU.is_le, tag="kept")
    kb.tt("dve", pfTD(bs, 0, T, 0, 32), pfTD(bs, 0, T, 0, 32), O01,
          ALU.mult, tag="gates")

    if L2B_PHASE < 4:
        return
    for ch in range(NCH):
        kb.memset("pool", PBF.t(ch), 0.0, tag="facc_z")
    bcv = bs["PF32"].base().bitcast(BF16)
    WGU = lambda ci, c0, c1: bs["PF32"].wv(bcv[:, ci * 1024 + c0:
                                               ci * 1024 + c1],
                                           0, 128, (ci * 1024 + c0) // 2,
                                           (ci * 1024 + c1) // 2)
    WDN = lambda ii, c0, c1: bs["PF32"].wv(bcv[:, 8192 + ii * 1024 + c0:
                                               8192 + ii * 1024 + c1],
                                           0, 128, (8192 + ii * 1024 + c0) // 2,
                                           (8192 + ii * 1024 + c1) // 2)
    for e in range(33):
        shared = (e == 32)
        if not shared:
            kb.dma("pool", WS.f(0, T, 0, 1), pfTD(bs, 0, T, e, e + 1),
                   tag="g_row")
        wg = Wd["sgu"] if shared else Wd["wgu"]
        wd_ = Wd["sdn"] if shared else Wd["wdn"]
        r0 = 0 if shared else e * D
        r1 = 0 if shared else e * 512
        gsrc = wg.base()[r0:r0 + D, :].rearrange("(c p) n -> p c n", p=128)
        kb.dma("pool", bs["PF32"].wv(bcv[:, 0:8192].rearrange(
            "p (c n) -> p c n", c=8), 0, 128, 0, 4096), wg.wv(gsrc),
            tag="wgu_ld")
        dsrc = wd_.base()[r1:r1 + 512, :].rearrange("(c p) n -> p c n", p=128)
        kb.dma("pool", bs["PF32"].wv(bcv[:, 8192:12288].rearrange(
            "p (c n) -> p c n", c=4), 0, 128, 4096, 6144), wd_.wv(dsrc),
            tag="wdn_ld")
        for tc in range(4):
            t0, t1 = tc * 512, (tc + 1) * 512
            if not shared:
                kb.mm(PB.f(0, 512), onesr.f(), WS.f(t0, t1, 0, 1),
                      start=True, stop=True, tag="g_bc")
                kb.copy("dve", WS.f(2048, 2560), PB.f(0, 512), tag="g_bcc")
            for ii in range(4):
                for ci in range(NCH):
                    kb.mm(PM0.f(0, 512), WGU(ci, ii * 128, (ii + 1) * 128),
                          XN.t(ci, t0, t1), start=(ci == 0),
                          stop=(ci == NCH - 1), tag="gu_mm")
                    kb.mm(PM1.f(0, 512), WGU(ci, 512 + ii * 128,
                                             512 + (ii + 1) * 128),
                          XN.t(ci, t0, t1), start=(ci == 0),
                          stop=(ci == NCH - 1), tag="up_mm")
                kb.act(HB2.f(ii * 512, (ii + 1) * 512), PM0.f(0, 512),
                       AF.Silu, tag="gu_si")
                kb.tt("dve", HB2.f(ii * 512, (ii + 1) * 512),
                      HB2.f(ii * 512, (ii + 1) * 512), PM1.f(0, 512),
                      ALU.mult, tag="gu_h")
                if not shared:
                    kb.tt("dve", HB2.f(ii * 512, (ii + 1) * 512),
                          HB2.f(ii * 512, (ii + 1) * 512), WS.f(2048, 2560),
                          ALU.mult, tag="gu_g")
            for co in range(NCH):
                pm = PM0 if co % 2 == 0 else PM1
                for ii in range(4):
                    kb.mm(pm.f(0, 512), WDN(ii, co * 128, (co + 1) * 128),
                          HB2.f(ii * 512, (ii + 1) * 512), start=(ii == 0),
                          stop=(ii == 3), tag="dn_mm")
                kb.tt("dve", PBF.t(co, t0, t1), PBF.t(co, t0, t1),
                      pm.f(0, 512), ALU.add, tag="dn_acc")

    if L2B_PHASE < 5:
        return
    for dj in range(NCH):
        for tc in range(4):
            t0, t1 = tc * 512, (tc + 1) * 512
            kb.act(pfTA(bs, 0, 512), PBF.t(dj, t0, t1), AF.Silu,
                   tag="res2_silu")
            kb.tt("dve", pfTA(bs, 0, 512), pfTA(bs, 0, 512),
                  ACC.t(dj, t0, t1), ALU.mult, tag="res2_mul")
            kb.tt("dve", X.t(dj, t0, t1), X.t(dj, t0, t1), pfTA(bs, 0, 512),
                  ALU.add, tag="res2_add")


def build_l2a():
    kb = KB()
    xin = kb.dram("x_in", [T, D], F32)
    ow = kb.dram("ow_out", [T, 32], F32, kind="ExternalOutput")
    cnt = kb.dram("cnt_out", [1, 32], F32, kind="ExternalOutput")
    gw = kb.dram("L2_gw", [32, D], F32)
    cm = setup_common(kb)
    bs = make_bufs(kb)
    emit_load_x_fm(kb, bs, cm, xin, bs["X"])
    emit_l2a(kb, bs, cm, {"gw": gw}, ow, cnt)
    nc = kb.build()
    return kb, nc


def build_l2b():
    kb = KB()
    xin = kb.dram("x_in", [T, D], F32)
    xout = kb.dram("x_out", [T, D], F32, kind="ExternalOutput")
    owt = kb.dram("owt", [32, T], F32)
    cnt8 = kb.dram("cnt8", [8, 32], F32)
    pmask = kb.dram("pmask", [8, 1], F32)
    Wd = {nm: kb.dram("L2_" + nm, shp, F32) for nm, shp in L2_SHAPES.items()}
    cm = setup_common(kb)
    bs = make_bufs(kb)
    emit_load_x_fm(kb, bs, cm, xin, bs["X"])
    emit_l2b(kb, bs, cm, Wd, owt, cnt8, pmask)
    emit_store_x(kb, bs, cm, bs["X"], xout)
    nc = kb.build()
    return kb, nc


def l2_inmap(x_b, ins, owt=None, cnt8=None, core=0, for_a=False):
    if for_a:
        return {"x_in": np.ascontiguousarray(x_b),
                "L2_gw": np.ascontiguousarray(
                    np.asarray(ins["L2_ffn_gate_w"], dtype=np.float32))}
    m = {"x_in": np.ascontiguousarray(x_b),
         "owt": np.ascontiguousarray(owt),
         "cnt8": np.ascontiguousarray(cnt8),
         "pmask": np.ascontiguousarray(
             (np.arange(8) < core).astype(np.float32).reshape(8, 1))}
    amap = {"cvw": "L2_attn_conv_weight", "cvb": "L2_attn_conv_bias",
            "Wrw": "L2_attn_Wr_w", "Wrb": "L2_attn_Wr_b",
            "Wiw": "L2_attn_Wi_w", "Wib": "L2_attn_Wi_b",
            "loga": "L2_attn_log_a", "outw": "L2_attn_out_w",
            "nw": "L2_attn_norm_w", "gw": "L2_ffn_gate_w",
            "wgu": "L2_ffn_w_gate_up", "wdn": "L2_ffn_w_down",
            "sgu": "L2_ffn_shared_w_gate_up", "sdn": "L2_ffn_shared_w_down"}
    for nm, key in amap.items():
        v = np.ascontiguousarray(np.asarray(ins[key], dtype=np.float32))
        m["L2_" + nm] = v.reshape(L2_SHAPES[nm])
    return m



_CACHE = {}


def _get(name, builder):
    if name not in _CACHE:
        _CACHE[name] = builder()
    return _CACHE[name]


def kernel(**inputs):
    ins = {k: np.asarray(v) for k, v in inputs.items()}
    x = np.ascontiguousarray(np.asarray(ins["L0_attn_x"], dtype=np.float32))
    B = x.shape[0]
    cores = list(range(B))

    kb0, nc0 = _get("l0", build_l0)
    maps = [l0_inmap(x[b], ins) for b in range(B)]
    res = run_bass_kernel_spmd(nc0, maps, core_ids=cores)
    x = np.stack([res.results[b]["x_out"] for b in range(B)])

    kb1, nc1 = _get("l1", build_l1)
    maps = [l1_inmap(x[b], ins) for b in range(B)]
    res = run_bass_kernel_spmd(nc1, maps, core_ids=cores)
    x = np.stack([res.results[b]["x_out"] for b in range(B)])

    kba, nca = _get("l2a", build_l2a)
    maps = [l2_inmap(x[b], ins, for_a=True) for b in range(B)]
    res = run_bass_kernel_spmd(nca, maps, core_ids=cores)
    ows = [res.results[b]["ow_out"] for b in range(B)]
    cnt8 = np.ascontiguousarray(
        np.stack([res.results[b]["cnt_out"][0] for b in range(B)]))
    kbb, ncb = _get("l2b", build_l2b)
    maps = [l2_inmap(x[b], ins, owt=np.ascontiguousarray(ows[b].T),
                     cnt8=cnt8, core=b) for b in range(B)]
    res = run_bass_kernel_spmd(ncb, maps, core_ids=cores)
    x = np.stack([res.results[b]["x_out"] for b in range(B)])

    kb3, nc3 = _get("l3", build_l3)
    rtab = make_rtab()
    maps = [l3_inmap(x[b], ins, rtab) for b in range(B)]
    res = run_bass_kernel_spmd(nc3, maps, core_ids=cores)
    x = np.stack([res.results[b]["x_out"] for b in range(B)])
    return x
